# version 24
# baseline (speedup 1.0000x reference)
import contextlib
import numpy as np
import concourse.bass as bass
import concourse.mybir as mybir
from concourse.bass_utils import run_bass_kernel_spmd

F32 = mybir.dt.float32
BF16 = mybir.dt.bfloat16
AF = mybir.ActivationFunctionType
ALU = mybir.AluOpType
AX = mybir.AxisListType

D = 1024
HID = 2048
I_FF = 2816
TOK = 4096
NT_A = 8
TT_A = 512
NT_B = 4
TT_B = 1024
NC_D = 8
NC_H = 16
NC_I = 22
NEG_INF = -1e30
EPS = 1e-5


class Ctr:
    __slots__ = ("sem", "n")

    def __init__(self, sem):
        self.sem = sem
        self.n = 0


def inc(ins, c, k=1):
    ins.then_inc(c.sem, k)
    c.n += k
    return c.n


DEBUG = False


def build():
    nc = bass.Bass("TRN2", target_bir_lowering=False)

    x_tm = nc.dram_tensor("x_tm", [TOK, D], F32, kind="ExternalInput").ap()
    x_fm = nc.dram_tensor("x_fm", [D, TOK], F32, kind="ExternalInput").ap()
    wf_p = nc.dram_tensor("wf_p", [128, 49152], F32, kind="ExternalInput").ap()
    wo_p = nc.dram_tensor("wo_p", [128, 16384], F32, kind="ExternalInput").ap()
    g_p = nc.dram_tensor("g_p", [8, 128, NC_I * 1024], F32, kind="ExternalInput").ap()
    u_p = nc.dram_tensor("u_p", [8, 128, NC_I * 1024], F32, kind="ExternalInput").ap()
    d_p = nc.dram_tensor("d_p", [8, 128, NC_I * 1024], F32, kind="ExternalInput").ap()
    wr_p = nc.dram_tensor("wr_p", [128, 56], F32, kind="ExternalInput").ap()
    rb_p = nc.dram_tensor("rb_p", [1, 7], F32, kind="ExternalInput").ap()
    eye_p = nc.dram_tensor("eye_p", [128, 128], F32, kind="ExternalInput").ap()
    y_out = nc.dram_tensor("y_out", [TOK, D], F32, kind="ExternalOutput").ap()
    wf_st = nc.dram_tensor("wf_st", [128, 49152], BF16, kind="Internal").ap()
    wo_st = nc.dram_tensor("wo_st", [128, 16384], BF16, kind="Internal").ap()
    xn_st = nc.dram_tensor("xn_st", [128, NT_A * 8 * TT_A], BF16, kind="Internal").ap()
    y1_st = nc.dram_tensor("y1_st", [TOK, D], F32, kind="Internal").ap()
    if DEBUG:
        dbg_xn = nc.dram_tensor("dbg_xn", [128, 32768], BF16, kind="ExternalOutput").ap()
        dbg_y1 = nc.dram_tensor("dbg_y1", [TOK, D], F32, kind="ExternalOutput").ap()
        dbg_w = nc.dram_tensor("dbg_w", [128, 224], F32, kind="ExternalOutput").ap()
        dbg_h = nc.dram_tensor("dbg_h", [128, 16 * 513], BF16, kind="ExternalOutput").ap()
        dbg_t = nc.dram_tensor("dbg_t", [128, NC_I * 1024], BF16, kind="ExternalOutput").ap()
        dbg_dl = nc.dram_tensor("dbg_dl", [128, NC_I * 1024], BF16, kind="ExternalOutput").ap()

    with contextlib.ExitStack() as top:
        TE = top.enter_context
        sPre = Ctr(TE(nc.semaphore("sPre")))
        sP0L = [Ctr(TE(nc.semaphore(f"sP0L{i}"))) for i in range(2)]
        sP0S = [Ctr(TE(nc.semaphore(f"sP0S{i}"))) for i in range(2)]
        sXi2 = [Ctr(TE(nc.semaphore(f"sXi{i}"))) for i in range(2)]
        sXn = Ctr(TE(nc.semaphore("sXn")))
        sY1 = Ctr(TE(nc.semaphore("sY1")))
        sWfi = [Ctr(TE(nc.semaphore(f"sWfi{i}"))) for i in range(2)]
        sWo = Ctr(TE(nc.semaphore("sWo")))
        sGu = [Ctr(TE(nc.semaphore(f"sGu{i}"))) for i in range(2)]
        sDd = [Ctr(TE(nc.semaphore(f"sDd{i}"))) for i in range(2)]
        sBio = Ctr(TE(nc.semaphore("sBio")))
        sYB = Ctr(TE(nc.semaphore("sYB")))
        sOut = Ctr(TE(nc.semaphore("sOut")))
        vT = Ctr(TE(nc.semaphore("vT")))
        vV = Ctr(TE(nc.semaphore("vV")))
        vS = Ctr(TE(nc.semaphore("vS")))
        vP = Ctr(TE(nc.semaphore("vP")))

        pb = [TE(nc.psum_tensor(f"pb{i}", [128, 512], F32)) for i in range(8)]

        eye_s = TE(nc.sbuf_tensor("eye_s", [128, 128], F32))
        ones_r = TE(nc.sbuf_tensor("ones_r", [1, 128], F32))
        ones_c = TE(nc.sbuf_tensor("ones_c", [128, 1], F32))
        wr_s = TE(nc.sbuf_tensor("wr_s", [128, 56], F32))
        rb_r = TE(nc.sbuf_tensor("rb_r", [1, 7], F32))
        s_row = TE(nc.sbuf_tensor("s_row", [1, 7], F32))
        s_bc = TE(nc.sbuf_tensor("s_bc", [128, 7], F32))
        rb_bc = TE(nc.sbuf_tensor("rb_bc", [128, 7], F32))
        w_sb = TE(nc.sbuf_tensor("w_sb", [128, 224], F32))
        zcol = TE(nc.sbuf_tensor("zcol", [128, 1], F32))
        block = TE(nc.Block())

        @block.gpsimd
        def _(g):
            g.memset(ones_r[:], 1.0)
            g.memset(zcol[:], 0.0)
            mm = g.memset(ones_c[:], 1.0)
            inc(mm, vP)
        const_done = vP.n

        @block.sync
        def _(s):
            inc(s.dma_start(out=eye_s[:], in_=eye_p), sPre, 16)
            inc(s.dma_start(out=wr_s[:], in_=wr_p), sPre, 16)
            inc(s.dma_start(out=rb_r[:], in_=rb_p), sPre, 16)
        pre_dma = sPre.n

        with contextlib.ExitStack() as ph0:
            E0 = ph0.enter_context
            p0_stage = E0(nc.sbuf_tensor("p0_stage", [128, 8192], F32))
            p0_bf = E0(nc.sbuf_tensor("p0_bf", [128, 8192], BF16))
            p0_chunks = [(wf_p, wf_st, i * 4096) for i in range(12)] + \
                        [(wo_p, wo_st, i * 4096) for i in range(4)]
            p0_cast = []
            p0_dma = []
            p0_store = []
            for k, (src, dst, off) in enumerate(p0_chunks):
                o = (k % 2) * 4096

                @block.sync
                def _(s, k=k, o=o, src=src, off=off):
                    if k >= 2:
                        s.wait_ge(vV.sem, p0_cast[k - 2])
                    inc(s.dma_start(out=p0_stage[:, o:o + 4096],
                                    in_=src[:, off:off + 4096]), sP0L[k % 2], 16)
                    p0_dma.append(sP0L[k % 2].n)

                @block.vector
                def _(v, k=k, o=o):
                    v.wait_ge(sP0L[k % 2].sem, p0_dma[k])
                    if k >= 2:
                        v.wait_ge(sP0S[k % 2].sem, p0_store[k - 2])
                    mm = v.tensor_copy(out=p0_bf[:, o:o + 4096],
                                       in_=p0_stage[:, o:o + 4096])
                    p0_cast.append(inc(mm, vV))

                @block.gpsimd
                def _(g, k=k, o=o, dst=dst, off=off):
                    g.wait_ge(vV.sem, p0_cast[k])
                    inc(g.dma_start(out=dst[:, off:off + 4096],
                                    in_=p0_bf[:, o:o + 4096]), sP0S[k % 2], 16)
                    p0_store.append(sP0S[k % 2].n)
            p0_store_all = (sP0S[0].n, sP0S[1].n)

            @block.tensor
            def _(t):
                t.wait_ge(sPre.sem, pre_dma)
                t.wait_ge(vP.sem, const_done)
                for c in range(8):
                    t.matmul(out=pb[6][0:1, 0:7], lhsT=ones_c[:],
                             rhs=wr_s[:, c * 7:c * 7 + 7],
                             start=(c == 0), stop=(c == 7))
                mm = t.matmul(out=pb[7][:, 0:7], lhsT=ones_r[:], rhs=rb_r[0:1, 0:7],
                              start=True, stop=True)
                inc(mm, vT)
            pre_t1 = vT.n

            @block.vector
            def _(v):
                v.wait_ge(vT.sem, pre_t1)
                v.tensor_copy(out=s_row[:], in_=pb[6][0:1, 0:7])
                mm = v.tensor_copy(out=rb_bc[:], in_=pb[7][:, 0:7])
                inc(mm, vV)
            pre_v1 = vV.n

            @block.tensor
            def _(t):
                t.wait_ge(vV.sem, pre_v1)
                mm = t.matmul(out=pb[7][:, 0:7], lhsT=ones_r[:], rhs=s_row[0:1, 0:7],
                              start=True, stop=True)
                inc(mm, vT)
            pre_t2 = vT.n

            @block.vector
            def _(v):
                v.wait_ge(vT.sem, pre_t2)
                mm = v.tensor_copy(out=s_bc[:], in_=pb[7][:, 0:7])
                inc(mm, vV)

        with contextlib.ExitStack() as ph_a:
            E = ph_a.enter_context
            xtm = [E(nc.sbuf_tensor(f"xtm{i}", [128, 4096], F32)) for i in range(2)]
            xfm = [E(nc.sbuf_tensor(f"xfm{i}", [128, 4096], F32)) for i in range(2)]
            xnfm = E(nc.sbuf_tensor("xnfm", [128, 4096], BF16))
            h_bf = E(nc.sbuf_tensor("h_bf", [128, NC_H * 513], BF16))
            y_sb = E(nc.sbuf_tensor("y_sb", [128, 4096], F32))
            wfb = [E(nc.sbuf_tensor(f"wfb{i}", [128, 3072], BF16)) for i in range(2)]
            wob16 = E(nc.sbuf_tensor("wob16", [128, 16 * 512], BF16))
            sq_sc = E(nc.sbuf_tensor("sq_sc", [128, 2048], F32))
            negm = E(nc.sbuf_tensor("negm", [128, 4], F32))
            sums = E(nc.sbuf_tensor("sums", [128, 4], F32))
            rstd = E(nc.sbuf_tensor("rstd", [128, 4], F32))
            tc1 = E(nc.sbuf_tensor("tc1", [128, 4], F32))
            nm_row = E(nc.sbuf_tensor("nm_row", [1, 512], F32))
            rs_row = E(nc.sbuf_tensor("rs_row", [1, 512], F32))
            l8 = E(nc.sbuf_tensor("l8", [128, 32], F32))
            m8 = E(nc.sbuf_tensor("m8", [128, 32], F32))
            nm1 = E(nc.sbuf_tensor("nm1", [128, 4], F32))
            dty = E(nc.sbuf_tensor("dty", [128, 4], F32))
            rdE = E(nc.sbuf_tensor("rdE", [128, 4], F32))
            e28 = E(nc.sbuf_tensor("e28", [128, 28], F32))
            mk28 = E(nc.sbuf_tensor("mk28", [128, 28], F32))
            asc = E(nc.sbuf_tensor("asc", [128, 2048], F32))
            easb = [E(nc.sbuf_tensor(f"easb{i}", [128, 512], F32)) for i in range(2)]
            ebsb = [E(nc.sbuf_tensor(f"ebsb{i}", [128, 512], F32)) for i in range(2)]
            rzsb = E(nc.sbuf_tensor("rzsb", [128, 512], F32))
            ezsb = [E(nc.sbuf_tensor(f"ezsb{i}", [128, 512], F32)) for i in range(2)]
            hrsb = [E(nc.sbuf_tensor(f"hrsb{i}", [128, 512], F32)) for i in range(2)]
            densb = E(nc.sbuf_tensor("densb", [128, 512], F32))
            rdenb = E(nc.sbuf_tensor("rdenb", [128, 512], F32))
            isb = [E(nc.sbuf_tensor(f"isb{i}", [128, 512], F32)) for i in range(2)]
            den2b = E(nc.sbuf_tensor("den2b", [128, 512], F32))
            rd2b = [E(nc.sbuf_tensor(f"rd2b{i}", [128, 512], F32)) for i in range(2)]
            szsb = [E(nc.sbuf_tensor(f"szsb{i}", [128, 512], F32)) for i in range(2)]
            gsb = E(nc.sbuf_tensor("gsb", [128, 512], F32))
            f_bu = [E(nc.sbuf_tensor(f"f_bu{i}", [128, 513], F32)) for i in range(2)]
            v_bu = [E(nc.sbuf_tensor(f"v_bu{i}", [128, 513], F32)) for i in range(2)]

            @block.gpsimd
            def _(g):
                mm = g.memset(l8[:], NEG_INF)
                inc(mm, vP)
                g.memset(f_bu[0][:, 0:1], 0.0)
                mm = g.memset(f_bu[1][:, 0:1], 0.0)
                inc(mm, vP)
            amemset_done = vP.n

            x_loaded = [0] * NT_A
            xtm_free = [0, 0]
            xfm_free = [0, 0]
            xn_ready = [0] * NT_A
            xn_stored = [0] * NT_A
            y1_stored = [0] * NT_A
            wf_dma = {}
            pe_iter = {}
            fg_free = {}
            hid_free = {}
            mark = {}
            hin_w = [0] * 16
            var_done_a = [0] * NT_A
            rstd_done_a = [0] * NT_A
            tp_done_a = [0] * NT_A
            bc_done_a = [0] * NT_A
            rt_done_a = [0] * NT_A
            h_all_a = [0] * NT_A
            wo_pe = {}
            ao_evac = {}

            def emit_xload(t):
                b = t % 2

                @block.gpsimd
                def _(g, t=t, b=b):
                    if t < 2:
                        g.wait_ge(sP0S[0].sem, p0_store_all[0])
                        g.wait_ge(sP0S[1].sem, p0_store_all[1])
                        g.wait_ge(vV.sem, p0_cast[-1])
                    w = max(xtm_free[b], xfm_free[b])
                    if w:
                        g.wait_ge(vV.sem, w)
                    inc(g.dma_start(
                        out=xtm[b][:].rearrange("p (j d) -> p j d", j=4),
                        in_=x_tm[t * TT_A:(t + 1) * TT_A, :]
                        .rearrange("(j p) d -> p j d", p=128)), sXi2[b], 16)
                    inc(g.dma_start(
                        out=xfm[b][:].rearrange("p (c n) -> p c n", c=8),
                        in_=x_fm[:, t * TT_A:(t + 1) * TT_A]
                        .rearrange("(c p) n -> p c n", p=128)), sXi2[b], 16)
                    x_loaded[t] = sXi2[b].n

            def emit_ln(t):
                b = t % 2

                @block.scalar
                def _(s, t=t, b=b):
                    s.wait_ge(sXi2[b].sem, x_loaded[t])
                    mm = None
                    for j in range(4):
                        if j == 2:
                            s.wait_ge(vS.sem, inc(mm, vS))
                        mm = s.activation(out=sq_sc[:, (j % 2) * 1024:
                                              (j % 2 + 1) * 1024],
                                          in_=xtm[b][:, j * 1024:(j + 1) * 1024],
                                          func=AF.Square, accum_out=sums[:, j:j + 1])
                    inc(mm, vS)
                sums_done = vS.n

                @block.vector
                def _(v, t=t, b=b):
                    v.wait_ge(sXi2[b].sem, x_loaded[t])
                    if t > 0:
                        v.wait_ge(vT.sem, tp_done_a[t - 1])
                    mm = None
                    for j in range(4):
                        mm = v.reduce_sum(out=negm[:, j:j + 1],
                                          in_=xtm[b][:, j * 1024:(j + 1) * 1024],
                                          axis=AX.X, negate=True)
                    v.wait_ge(vV.sem, inc(mm, vV))
                    mm = v.tensor_scalar(out=negm[:], in0=negm[:], scalar1=1.0 / D,
                                         scalar2=None, op0=ALU.mult)
                    v.wait_ge(vV.sem, inc(mm, vV))
                    mm = v.tensor_tensor(out=tc1[:], in0=negm[:], in1=negm[:],
                                         op=ALU.mult)
                    v.wait_ge(vV.sem, inc(mm, vV))
                    v.wait_ge(vS.sem, sums_done)
                    mm = v.scalar_tensor_tensor(out=tc1[:], in0=sums[:],
                                                scalar=1.0 / D, in1=tc1[:],
                                                op0=ALU.mult, op1=ALU.subtract)
                    v.wait_ge(vV.sem, inc(mm, vV))
                    mm = v.tensor_scalar(out=tc1[:], in0=tc1[:], scalar1=EPS,
                                         scalar2=None, op0=ALU.add)
                    inc(mm, vV)
                var_done_a[t] = vV.n

                @block.scalar
                def _(s, t=t):
                    s.wait_ge(vV.sem, var_done_a[t])
                    mm = s.activation(out=rstd[:], in_=tc1[:], func=AF.Ln)
                    s.wait_ge(vS.sem, inc(mm, vS))
                    mm = s.activation(out=rstd[:], in_=rstd[:], func=AF.Exp,
                                      scale=-0.5)
                    inc(mm, vS)
                rstd_done_a[t] = vS.n

            def emit_tpbc(t):
                @block.tensor
                def _(t_, t=t):
                    t_.wait_ge(vS.sem, rstd_done_a[t])
                    t_.wait_ge(vV.sem, var_done_a[t])
                    for j in range(4):
                        t_.transpose(out=pb[6][0:1, j * 128:(j + 1) * 128],
                                     in_=negm[:, j:j + 1], identity=eye_s[:])
                    for j in range(4):
                        mm = t_.transpose(out=pb[7][0:1, j * 128:(j + 1) * 128],
                                          in_=rstd[:, j:j + 1], identity=eye_s[:])
                    inc(mm, vT)
                tp_done_a[t] = vT.n

                @block.vector
                def _(v, t=t):
                    v.wait_ge(vT.sem, tp_done_a[t])
                    v.tensor_copy(out=nm_row[:], in_=pb[6][0:1, :])
                    mm = v.tensor_copy(out=rs_row[:], in_=pb[7][0:1, :])
                    inc(mm, vV)
                rows_done = vV.n

                @block.tensor
                def _(t_, t=t, rows_done=rows_done):
                    t_.wait_ge(vV.sem, rows_done)
                    t_.matmul(out=pb[6][:], lhsT=ones_r[:], rhs=nm_row[:],
                              start=True, stop=True)
                    mm = t_.matmul(out=pb[7][:], lhsT=ones_r[:], rhs=rs_row[:],
                                   start=True, stop=True)
                    inc(mm, vT)
                bc_done_a[t] = vT.n

            def emit_xn(t):
                b = t % 2

                @block.vector
                def _(v, t=t, b=b):
                    v.wait_ge(vT.sem, bc_done_a[t])
                    if t > 0:
                        v.wait_ge(sXn.sem, xn_stored[t - 1])
                    mm = None
                    for grp in range(2):
                        for c in range(grp * 4, grp * 4 + 4):
                            sl = slice(c * TT_A, (c + 1) * TT_A)
                            a_sl = slice((c % 4) * TT_A, (c % 4 + 1) * TT_A)
                            mm = v.tensor_tensor(out=asc[:, a_sl],
                                                 in0=xfm[b][:, sl], in1=pb[6][:],
                                                 op=ALU.add)
                        v.wait_ge(vV.sem, inc(mm, vV))
                        for c in range(grp * 4, grp * 4 + 4):
                            sl = slice(c * TT_A, (c + 1) * TT_A)
                            a_sl = slice((c % 4) * TT_A, (c % 4 + 1) * TT_A)
                            mm = v.tensor_tensor(out=xnfm[:, sl], in0=asc[:, a_sl],
                                                 in1=pb[7][:], op=ALU.mult)
                        if grp == 0:
                            v.wait_ge(vV.sem, inc(mm, vV))
                    inc(mm, vV)
                xn_ready[t] = vV.n

                @block.gpsimd
                def _(g, t=t):
                    g.wait_ge(vV.sem, xn_ready[t])
                    inc(g.dma_start(out=xn_st[:, t * 4096:(t + 1) * 4096],
                                    in_=xnfm[:]), sXn, 16)
                    xn_stored[t] = sXn.n

            def emit_router(t):
                b = t % 2

                @block.tensor
                def _(t_, t=t, b=b):
                    t_.wait_ge(vV.sem, xn_ready[t])
                    mm = None
                    for g in range(4):
                        for c in range(NC_D):
                            mm = t_.matmul(
                                out=pb[6][:, g * 8:g * 8 + 7],
                                lhsT=xfm[b][:, c * TT_A + g * 128:
                                            c * TT_A + g * 128 + 128],
                                rhs=wr_s[:, c * 7:c * 7 + 7],
                                start=(c == 0), stop=(c == NC_D - 1))
                    inc(mm, vT)
                rt_done_a[t] = vT.n

                @block.vector
                def _(v, t=t):
                    v.wait_ge(vT.sem, rt_done_a[t])
                    v.wait_ge(vP.sem, amemset_done)
                    mm = None
                    for g in range(4):
                        mm = v.scalar_tensor_tensor(out=l8[:, g * 8:g * 8 + 7],
                                                    in0=s_bc[:],
                                                    scalar=negm[:, g:g + 1],
                                                    in1=pb[6][:, g * 8:g * 8 + 7],
                                                    op0=ALU.mult, op1=ALU.add)
                    v.wait_ge(vV.sem, inc(mm, vV))
                    for g in range(4):
                        mm = v.tensor_single_scalar(out=l8[:, g * 8:g * 8 + 7],
                                                    in_=l8[:, g * 8:g * 8 + 7],
                                                    scalar=rstd[:, g:g + 1],
                                                    op=ALU.mult)
                    v.wait_ge(vV.sem, inc(mm, vV))
                    for g in range(4):
                        mm = v.tensor_tensor(out=l8[:, g * 8:g * 8 + 7],
                                             in0=l8[:, g * 8:g * 8 + 7],
                                             in1=rb_bc[:], op=ALU.add)
                    v.wait_ge(vV.sem, inc(mm, vV))
                    for g in range(4):
                        mm = v.max(out=m8[:, g * 8:g * 8 + 8],
                                   in_=l8[:, g * 8:g * 8 + 8])
                    v.wait_ge(vV.sem, inc(mm, vV))
                    for g in range(4):
                        ls = slice(g * 8, g * 8 + 7)
                        v.tensor_scalar(out=nm1[:, g:g + 1],
                                        in0=m8[:, g * 8:g * 8 + 1],
                                        scalar1=-1.0, scalar2=None, op0=ALU.mult)
                        v.tensor_tensor(out=dty[:, g:g + 1],
                                        in0=m8[:, g * 8 + 1:g * 8 + 2],
                                        in1=m8[:, g * 8:g * 8 + 1], op=ALU.subtract)
                        mm = v.tensor_single_scalar(out=mk28[:, g * 7:g * 7 + 7],
                                                    in_=l8[:, ls],
                                                    scalar=m8[:, g * 8 + 1:g * 8 + 2],
                                                    op=ALU.is_ge)
                    inc(mm, vV)
                rt_v = vV.n

                @block.scalar
                def _(s, t=t, rt_v=rt_v):
                    s.wait_ge(vV.sem, rt_v)
                    for g in range(4):
                        s.activation(out=e28[:, g * 7:g * 7 + 7],
                                     in_=l8[:, g * 8:g * 8 + 7], func=AF.Exp,
                                     bias=nm1[:, g:g + 1])
                    mm = s.activation(out=rdE[:], in_=dty[:], func=AF.Exp)
                    inc(mm, vS)
                rt_s = vS.n

                @block.vector
                def _(v, t=t, rt_s=rt_s):
                    v.wait_ge(vS.sem, rt_s)
                    mm = v.tensor_scalar(out=rdE[:], in0=rdE[:], scalar1=1.0,
                                         scalar2=None, op0=ALU.add)
                    v.wait_ge(vV.sem, inc(mm, vV))
                    mm = v.reciprocal(out=rdE[:], in_=rdE[:])
                    v.wait_ge(vV.sem, inc(mm, vV))
                    mm = None
                    for g in range(4):
                        mm = v.tensor_single_scalar(out=e28[:, g * 7:g * 7 + 7],
                                                    in_=e28[:, g * 7:g * 7 + 7],
                                                    scalar=rdE[:, g:g + 1],
                                                    op=ALU.mult)
                    v.wait_ge(vV.sem, inc(mm, vV))
                    mm = None
                    for g in range(4):
                        es = slice(g * 7, g * 7 + 7)
                        wc = (t * 4 + g) * 7
                        mm = v.tensor_tensor(out=w_sb[:, wc:wc + 7], in0=e28[:, es],
                                             in1=mk28[:, es], op=ALU.mult)
                    inc(mm, vV)
                xfm_free[b] = vV.n

            def emit_gates_iter(t, hp):
                idx = t * 16 + hp
                wbi = idx % 2
                trio = 3 * (idx % 2)

                @block.sync
                def _(s, idx=idx, wbi=wbi, hp=hp):
                    if idx == 0:
                        s.wait_ge(sP0S[0].sem, p0_store_all[0])
                        s.wait_ge(sP0S[1].sem, p0_store_all[1])
                    if idx >= 2:
                        s.wait_ge(vT.sem, pe_iter[idx - 2][2])
                    inc(s.dma_start(out=wfb[wbi][:],
                                    in_=wf_st[:, hp * 3072:hp * 3072 + 3072]),
                        sWfi[wbi], 16)
                    wf_dma[idx] = sWfi[wbi].n

                @block.tensor
                def _(t_, t=t, hp=hp, idx=idx, wbi=wbi, trio=trio):
                    t_.wait_ge(sWfi[wbi].sem, wf_dma[idx])
                    if hp == 0:
                        t_.wait_ge(vV.sem, xn_ready[t])
                    if hp <= 1 and t > 0:
                        t_.wait_ge(vV.sem, ao_evac[(t - 1, 1)])
                    if idx >= 2:
                        t_.wait_ge(vS.sem, fg_free[idx - 2])
                        t_.wait_ge(vV.sem, hid_free[idx - 2])
                        t_.wait_ge(vS.sem, mark[(idx - 2, "rd2")])
                    marks = []
                    for q in range(3):
                        mm = None
                        for c in range(NC_D):
                            mm = t_.matmul(
                                out=pb[trio + q][:],
                                lhsT=wfb[wbi][:, (q * 8 + c) * 128:
                                              (q * 8 + c) * 128 + 128],
                                rhs=xnfm[:, c * TT_A:(c + 1) * TT_A],
                                start=(c == 0), stop=(c == NC_D - 1))
                        marks.append(inc(mm, vT))
                    pe_iter[idx] = tuple(marks)

                q0, q1v, q2 = pe_iter[idx]

                @block.scalar
                def _(s, t=t, hp=hp, idx=idx, trio=trio, q0=q0, q1v=q1v, q2=q2):
                    par = idx % 2
                    if idx > 1:
                        s.wait_ge(vV.sem, mark[(idx - 2, "sgz")])
                        s.wait_ge(vP.sem, mark[(idx - 2, "g")])
                    s.wait_ge(vT.sem, q0)
                    s.activation(out=easb[par][:], in_=pb[trio][:], func=AF.Exp,
                                 scale=-1.0)
                    s.wait_ge(vT.sem, q1v)
                    mm = s.activation(out=ebsb[par][:], in_=pb[trio + 1][:],
                                      func=AF.Exp, scale=-1.0)
                    fg_free[idx] = inc(mm, vS)
                    s.wait_ge(vT.sem, q2)
                    mm = s.activation(out=rzsb[:], in_=pb[trio + 2][:], func=AF.Relu,
                                      scale=-1.0)
                    s.wait_ge(vS.sem, inc(mm, vS))
                    s.activation(out=ezsb[par][:], in_=rzsb[:], func=AF.Exp,
                                 scale=-1.0)
                    s.activation(out=den2b[:], in_=ezsb[par][:], func=AF.Ln,
                                 bias=1.0)
                    mm = s.activation(out=rd2b[par][:], in_=den2b[:], func=AF.Exp,
                                      scale=-1.0)
                    mark[(idx, "rd2")] = inc(mm, vS)
                    if idx > 1:
                        s.wait_ge(vV.sem, mark[(idx - 2, "scan")])
                    if t == 0 or t == 4:
                        s.wait_ge(vP.sem, const_done)
                        mm = s.copy(out=v_bu[idx % 2][:, 0:1], in_=zcol[:])
                    else:
                        if hin_w[hp]:
                            s.wait_ge(vV.sem, hin_w[hp])
                        mm = s.copy(out=v_bu[idx % 2][:, 0:1],
                                    in_=h_bf[:, hp * 513 + 512:hp * 513 + 513])
                    mark[(idx, "carry")] = inc(mm, vS)

                @block.vector
                def _(v, idx=idx, trio=trio, q2=q2):
                    par = idx % 2
                    if idx > 1:
                        v.wait_ge(vP.sem, mark[(idx - 2, "v")])
                    v.wait_ge(vS.sem, fg_free[idx])
                    mm = v.scalar_tensor_tensor(out=densb[:], in0=easb[par][:],
                                                scalar=2.0, in1=ebsb[par][:],
                                                op0=ALU.add, op1=ALU.add)
                    v.wait_ge(vV.sem, inc(mm, vV))
                    mm = v.reciprocal(out=rdenb[:], in_=densb[:])
                    v.wait_ge(vV.sem, inc(mm, vV))
                    mm = v.scalar_tensor_tensor(out=f_bu[par][:, 1:513],
                                                in0=ebsb[par][:],
                                                scalar=1.0, in1=rdenb[:],
                                                op0=ALU.add, op1=ALU.mult)
                    mark[(idx, "f")] = inc(mm, vV)
                    mm = v.scalar_tensor_tensor(out=isb[par][:], in0=easb[par][:],
                                                scalar=1.0, in1=rdenb[:],
                                                op0=ALU.add, op1=ALU.mult)
                    mark[(idx, "i")] = inc(mm, vV)
                    v.wait_ge(vT.sem, q2)
                    mm = v.tensor_scalar(out=hrsb[par][:], in0=pb[trio + 2][:],
                                         scalar1=0.0, scalar2=None, op0=ALU.max)
                    hid_free[idx] = inc(mm, vV)
                    mark[(idx, "sgz")] = vV.n

                @block.gpsimd
                def _(g, t=t, hp=hp, idx=idx):
                    par = idx % 2
                    g.wait_ge(vV.sem, hid_free[idx])
                    g.wait_ge(vS.sem, mark[(idx, "rd2")])
                    mm = g.tensor_tensor(out=szsb[par][:], in0=ezsb[par][:],
                                         in1=rd2b[par][:], op=ALU.mult)
                    cnt = inc(mm, vP)
                    g.wait_ge(vP.sem, cnt)
                    mm = g.tensor_tensor(out=gsb[:], in0=szsb[par][:],
                                         in1=hrsb[par][:], op=ALU.add)
                    mark[(idx, "g")] = inc(mm, vP)
                    g.wait_ge(vP.sem, mark[(idx, "g")])
                    if idx > 1:
                        g.wait_ge(vV.sem, mark[(idx - 2, "scan")])
                    mm = g.tensor_tensor(out=v_bu[par][:, 1:513], in0=isb[par][:],
                                         in1=gsb[:], op=ALU.mult)
                    mark[(idx, "v")] = inc(mm, vP)

                def emit_scan(j, t=t):
                    jp = j % 16

                    @block.vector
                    def _(v, j=j, jp=jp):
                        v.wait_ge(vP.sem, mark[(j, "v")])
                        v.wait_ge(vS.sem, mark[(j, "carry")])
                        mm = v.tensor_tensor_scan(out=h_bf[:, jp * 513:
                                                         (jp + 1) * 513],
                                                  data0=f_bu[j % 2][:],
                                                  data1=v_bu[j % 2][:], initial=0.0,
                                                  op0=ALU.mult, op1=ALU.add)
                        mark[(j, "scan")] = inc(mm, vV)
                        hin_w[jp] = mark[(j, "scan")]

                if hp > 0:
                    emit_scan(idx - 1)
                if hp == 15:
                    emit_scan(idx)
                    h_all_a[t] = vV.n

            def emit_attn_out(t, half):
                b = t % 2

                @block.sync
                def _(s, t=t, half=half):
                    if half == 0:
                        if t > 0:
                            s.wait_ge(vT.sem, wo_pe[(t - 1, 1)])
                        else:
                            s.wait_ge(sP0S[0].sem, p0_store_all[0])
                            s.wait_ge(sP0S[1].sem, p0_store_all[1])
                    else:
                        s.wait_ge(vT.sem, wo_pe[(t, 0)])
                    for hc in range(NC_H):
                        inc(s.dma_start(
                            out=wob16[:, hc * 512:(hc + 1) * 512],
                            in_=wo_st[:, (hc * 2 + half) * 512:
                                      (hc * 2 + half) * 512 + 512]), sWo, 16)
                    wf_dma[(t, half, "wo")] = sWo.n

                @block.tensor
                def _(t_, t=t, half=half):
                    t_.wait_ge(sWo.sem, wf_dma[(t, half, "wo")])
                    if half == 0:
                        t_.wait_ge(vS.sem, mark[(t * 16 + 15, "rd2")])
                        t_.wait_ge(vV.sem, hid_free[t * 16 + 15])
                        if t > 0:
                            t_.wait_ge(vV.sem, ao_evac[(t - 1, 1)])
                    else:
                        t_.wait_ge(vV.sem, ao_evac[(t, 0)])
                    mm = None
                    for hc in range(NC_H):
                        if half == 0:
                            t_.wait_ge(vV.sem, mark[(t * 16 + hc, "scan")])
                        for g in range(4):
                            mm = t_.matmul(
                                out=pb[g][:],
                                lhsT=h_bf[:, hc * 513 + 1 + g * 128:
                                          hc * 513 + 1 + g * 128 + 128],
                                rhs=wob16[:, hc * 512:(hc + 1) * 512],
                                start=(hc == 0), stop=(hc == NC_H - 1))
                    inc(mm, vT)
                    wo_pe[(t, half)] = vT.n
                ao_pe = vT.n

                @block.vector
                def _(v, t=t, b=b, half=half, ao_pe=ao_pe):
                    v.wait_ge(vT.sem, ao_pe)
                    if t > 0 and half == 0:
                        v.wait_ge(sY1.sem, y1_stored[t - 1])
                    mm = None
                    for g in range(4):
                        sl = slice(g * 1024 + half * 512,
                                   g * 1024 + half * 512 + 512)
                        mm = v.tensor_tensor(out=y_sb[:, sl], in0=xtm[b][:, sl],
                                             in1=pb[g][:], op=ALU.add)
                    ao_evac[(t, half)] = inc(mm, vV)
                if half == 1:
                    xtm_free[b] = vV.n

            def emit_ystore(t):
                @block.gpsimd
                def _(g, t=t):
                    g.wait_ge(vV.sem, ao_evac[(t, 0)])
                    g.wait_ge(vV.sem, ao_evac[(t, 1)])
                    inc(g.dma_start(out=y1_st[t * TT_A:(t + 1) * TT_A, :]
                                    .rearrange("(j p) d -> p j d", p=128),
                                    in_=y_sb[:].rearrange("p (j d) -> p j d", j=4)),
                        sY1, 16)
                    y1_stored[t] = sY1.n

            emit_xload(0)
            emit_xload(1)
            emit_ln(0)
            emit_tpbc(0)
            emit_xn(0)
            emit_router(0)

            for t in range(NT_A):
                for hp in range(16):
                    emit_gates_iter(t, hp)
                if t + 1 < NT_A:
                    emit_ln(t + 1)
                emit_attn_out(t, 0)
                if t + 1 < NT_A:
                    emit_tpbc(t + 1)
                    emit_xn(t + 1)
                emit_attn_out(t, 1)
                if t + 2 < NT_A:
                    emit_xload(t + 2)
                if t + 1 < NT_A:
                    emit_router(t + 1)
                emit_ystore(t)

            if DEBUG:
                @block.sync
                def _(s):
                    s.wait_ge(vS.sem, vS.n)
                    s.wait_ge(vV.sem, vV.n)
                    s.wait_ge(vP.sem, vP.n)
                    s.wait_ge(sXn.sem, sXn.n)
                    s.wait_ge(sY1.sem, sY1.n)
                    inc(s.dma_start(out=dbg_h, in_=h_bf[:]), sOut, 16)
                    inc(s.dma_start(out=dbg_w, in_=w_sb[:]), sOut, 16)
                    inc(s.dma_start(out=dbg_xn, in_=xn_st), sOut, 16)
                    inc(s.dma_start(out=dbg_y1, in_=y1_st), sOut, 16)

            a_V, a_S, a_P, a_T = vV.n, vS.n, vP.n, vT.n
            a_dma = [(sXn, sXn.n), (sY1, sY1.n), (sWo, sWo.n),
                     (sWfi[0], sWfi[0].n), (sWfi[1], sWfi[1].n),
                     (sXi2[0], sXi2[0].n), (sXi2[1], sXi2[1].n),
                     (sP0S[0], sP0S[0].n), (sP0S[1], sP0S[1].n)]

        for eng in ("vector", "scalar", "gpsimd", "tensor", "sync"):
            @getattr(block, eng)
            def _(e):
                e.wait_ge(vV.sem, a_V)
                e.wait_ge(vS.sem, a_S)
                e.wait_ge(vP.sem, a_P)
                e.wait_ge(vT.sem, a_T)
                for c_, n_ in a_dma:
                    e.wait_ge(c_.sem, n_)

        with contextlib.ExitStack() as ph_b:
            E = ph_b.enter_context
            xn_b = [E(nc.sbuf_tensor(f"xn_b{i}", [128, 8 * TT_B], BF16)) for i in range(2)]
            y_b = E(nc.sbuf_tensor("y_b", [128, 8 * 1024], F32))
            t_full = E(nc.sbuf_tensor("t_full", [128, NC_I * 1024], BF16))
            d_all = E(nc.sbuf_tensor("d_all", [128, NC_I * 1024], BF16))
            g_st = [E(nc.sbuf_tensor(f"g_st{i}", [128, 1024], F32)) for i in range(3)]
            u_st = [E(nc.sbuf_tensor(f"u_st{i}", [128, 1024], F32)) for i in range(3)]
            g_bf = [E(nc.sbuf_tensor(f"g_bf{i}", [128, 1024], BF16)) for i in range(3)]
            u_bf = [E(nc.sbuf_tensor(f"u_bf{i}", [128, 1024], BF16)) for i in range(3)]
            d_st = [E(nc.sbuf_tensor(f"d_st{i}", [128, 1024], F32)) for i in range(2)]
            s_sc = [E(nc.sbuf_tensor(f"s_sc{i}", [128, 512], F32)) for i in range(2)]
            sGu4 = [Ctr(TE(nc.semaphore(f"sGu4_{i}"))) for i in range(3)]

            sout_base = sOut.n
            io_done = [0] * NT_B
            gu_dma = {}
            gu_cast = {}
            gu_pe_last = {}
            unit_pe = {}
            unit_silu = {}
            unit_t = {}
            d_dma = {}
            d_cast = {}
            d_marks = []
            ddk = [0]
            t_war = 0
            d_war = 0
            prev_evac3 = 0
            xn_pe_last = [0] * NT_B
            uk = 0
            ick = 0
            dck = 0
            ic_total = 0

            def emit_gu_fetch(ick2, ic2, gsrc, usrc):
                sbi = ick2 % 3

                @block.sync
                def _(s):
                    inc(s.dma_start(out=g_st[sbi][:],
                                    in_=gsrc[:, ic2 * 1024:(ic2 + 1) * 1024]),
                        sGu4[sbi], 16)
                    inc(s.dma_start(out=u_st[sbi][:],
                                    in_=usrc[:, ic2 * 1024:(ic2 + 1) * 1024]),
                        sGu4[sbi], 16)
                    gu_dma[ick2] = sGu4[sbi].n

                @block.scalar
                def _(s):
                    s.wait_ge(sGu4[sbi].sem, gu_dma[ick2])
                    s.copy(out=g_bf[sbi][:], in_=g_st[sbi][:])
                    mm = s.copy(out=u_bf[sbi][:], in_=u_st[sbi][:])
                    gu_cast[ick2] = inc(mm, vS)

            xn_loaded_b = [0] * (NT_B + 1)
            y_loaded_b = [0] * NT_B

            def emit_xn_load_b(bt):
                @block.gpsimd
                def _(g, bt=bt):
                    if bt >= 2:
                        g.wait_ge(vT.sem, xn_pe_last[bt - 2])
                    for u in range(2):
                        inc(g.dma_start(
                            out=bass.AP(xn_b[bt % 2], u * 512,
                                        [[8192, 128], [1024, 8], [1, 512]]),
                            in_=xn_st[:, (2 * bt + u) * 4096:(2 * bt + u + 1) * 4096]
                            .rearrange("p (c s) -> p c s", c=8)), sBio, 16)
                    xn_loaded_b[bt] = sBio.n

            def emit_y_load_b(bt):
                @block.gpsimd
                def _(g, bt=bt):
                    if bt > 0:
                        g.wait_ge(sOut.sem, sout_base + 16 * bt)
                        g.wait_ge(vV.sem, y_evac_last)
                    inc(g.dma_start(
                        out=y_b[:].rearrange("p (j d) -> p j d", j=8),
                        in_=y1_st[bt * TT_B:(bt + 1) * TT_B, :]
                        .rearrange("(j p) d -> p j d", p=128)), sYB, 16)
                    y_loaded_b[bt] = sYB.n

            emit_xn_load_b(0)
            for bt in range(NT_B):
                emit_y_load_b(bt)
                io_done[bt] = 0

                srcs = [(g_p[e], u_p[e], d_p[e]) for e in range(8)]

                def glb(e_, ic_):
                    return bt * 8 * NC_I + e_ * NC_I + ic_

                def src_of(k):
                    e_, ic_ = divmod(k % (8 * NC_I), NC_I)
                    return srcs[e_][0], srcs[e_][1], ic_

                if bt == 0:
                    for k in range(2):
                        gs_, us_, ic_ = src_of(k)
                        emit_gu_fetch(k, ic_, gs_, us_)

                for e in range(8):
                    gsrc, usrc, dsrc = srcs[e]

                    def emit_d_dma(dsrc=dsrc, e=e):
                        gdk = ddk[0]
                        dk = gdk % 22
                        dbi = gdk % 2

                        @block.gpsimd
                        def _(g, gdk=gdk, dk=dk, dbi=dbi, dsrc=dsrc):
                            w = d_cast.get(gdk - 2)
                            if w:
                                g.wait_ge((vS if w[0] == "S" else vV).sem, w[1])
                            inc(g.dma_start(out=d_st[dbi][:],
                                            in_=dsrc[:, dk * 1024:(dk + 1) * 1024]),
                                sDd[dbi], 16)
                            d_dma[gdk] = sDd[dbi].n
                        ddk[0] += 1

                    emit_d_dma()
                    emit_d_dma()

                    for ic in range(NC_I):
                        k2 = glb(e, ic) + 2
                        has_fetch = k2 < 4 * 8 * NC_I
                        if has_fetch:
                            gs_, us_, ic2 = src_of(k2)
                            sb2 = k2 % 3

                            @block.sync
                            def _(s, k2=k2, sb2=sb2, ic2=ic2, gs_=gs_, us_=us_):
                                w = gu_cast.get(k2 - 3, 0)
                                if w:
                                    s.wait_ge(vS.sem, w)
                                inc(s.dma_start(out=g_st[sb2][:],
                                                in_=gs_[:, ic2 * 1024:(ic2 + 1) * 1024]),
                                    sGu4[sb2], 16)
                                inc(s.dma_start(out=u_st[sb2][:],
                                                in_=us_[:, ic2 * 1024:(ic2 + 1) * 1024]),
                                    sGu4[sb2], 16)
                                gu_dma[k2] = sGu4[sb2].n

                        ickg = glb(e, ic)
                        sbi = ickg % 3
                        uk0, uk1 = uk, uk + 1

                        for half in range(2):
                            bx = 2 * ((uk0 + half) % 2)
                            tok0 = half * 512

                            @block.tensor
                            def _(t_, bt=bt, e=e, ic=ic, ickg=ickg, uk=uk0 + half,
                                  sbi=sbi, half=half, bx=bx, tok0=tok0,
                                  pe3=prev_evac3):
                                if half == 0:
                                    t_.wait_ge(vS.sem, gu_cast[ickg])
                                if ic == 0 and half == 0:
                                    t_.wait_ge(sBio.sem, xn_loaded_b[bt])
                                if ic == 0 and pe3:
                                    t_.wait_ge(vV.sem, pe3)
                                w_s = unit_silu.get(uk - 2, 0)
                                w_t = unit_t.get(uk - 2, 0)
                                if w_s:
                                    t_.wait_ge(vS.sem, w_s)
                                if w_t:
                                    t_.wait_ge(vV.sem, w_t)
                                for c in range(NC_D):
                                    t_.matmul(out=pb[bx][:],
                                              lhsT=g_bf[sbi][:, c * 128:(c + 1) * 128],
                                              rhs=xn_b[bt % 2][:, c * TT_B + tok0:
                                                     c * TT_B + tok0 + 512],
                                              start=(c == 0), stop=(c == NC_D - 1))
                                mm = None
                                for c in range(NC_D):
                                    mm = t_.matmul(out=pb[bx + 1][:],
                                                   lhsT=u_bf[sbi][:,
                                                              c * 128:(c + 1) * 128],
                                                   rhs=xn_b[bt % 2][:,
                                                            c * TT_B + tok0:
                                                            c * TT_B + tok0 + 512],
                                                   start=(c == 0),
                                                   stop=(c == NC_D - 1))
                                unit_pe[uk] = inc(mm, vT)
                            gu_pe_last[ickg] = vT.n

                        bx0 = 2 * (uk0 % 2)
                        bx1 = 2 * (uk1 % 2)

                        @block.scalar
                        def _(s, bt=bt, e=e, ic=ic, uk0=uk0, uk1=uk1, bx0=bx0,
                              bx1=bx1, has_fetch=has_fetch,
                              k2=k2, dsrc=dsrc, dck=dck, d_war=d_war):
                            if has_fetch:
                                sb2 = k2 % 3
                                w = gu_pe_last.get(k2 - 3, 0)
                                if w:
                                    s.wait_ge(vT.sem, w)
                                s.wait_ge(sGu4[sb2].sem, gu_dma[k2])
                                s.copy(out=g_bf[sb2][:], in_=g_st[sb2][:])
                                mm = s.copy(out=u_bf[sb2][:], in_=u_st[sb2][:])
                                gu_cast[k2] = inc(mm, vS)
                            w = unit_t.get(uk0 - 2, 0)
                            if w:
                                s.wait_ge(vV.sem, w)
                            s.wait_ge(vT.sem, unit_pe[uk0])
                            mm = s.activation(out=s_sc[0][:], in_=pb[bx0][:],
                                              func=AF.Silu)
                            unit_silu[uk0] = inc(mm, vS)
                            w = unit_t.get(uk1 - 2, 0)
                            if w:
                                s.wait_ge(vV.sem, w)
                            s.wait_ge(vT.sem, unit_pe[uk1])
                            mm = s.activation(out=s_sc[1][:], in_=pb[bx1][:],
                                              func=AF.Silu)
                            unit_silu[uk1] = inc(mm, vS)
                            gdk = (bt * 8 + e) * 22 + ic
                            if gdk % 2 == 0:
                                s.wait_ge(sDd[0].sem, d_dma[gdk])
                                if ic == 0 and d_war:
                                    s.wait_ge(vT.sem, d_war)
                                mm = s.copy(out=d_all[:, ic * 1024:
                                                     (ic + 1) * 1024],
                                            in_=d_st[0][:])
                                d_cast[gdk] = ("S", inc(mm, vS))
                                d_marks.append(gdk)

                        @block.vector
                        def _(v, bt=bt, e=e, ic=ic, uk0=uk0, uk1=uk1, bx0=bx0,
                              bx1=bx1, t_war=t_war, d_war=d_war):
                            v.wait_ge(vS.sem, unit_silu[uk0])
                            if ic == 0 and t_war:
                                v.wait_ge(vT.sem, t_war)
                            mm = v.tensor_tensor(
                                out=t_full[:, ic * 1024:ic * 1024 + 512],
                                in0=s_sc[0][:], in1=pb[bx0 + 1][:], op=ALU.mult)
                            unit_t[uk0] = inc(mm, vV)
                            v.wait_ge(vS.sem, unit_silu[uk1])
                            mm = v.tensor_tensor(
                                out=t_full[:, ic * 1024 + 512:ic * 1024 + 1024],
                                in0=s_sc[1][:], in1=pb[bx1 + 1][:], op=ALU.mult)
                            unit_t[uk1] = inc(mm, vV)
                            gdk = (bt * 8 + e) * 22 + ic
                            if gdk % 2 == 1:
                                v.wait_ge(sDd[1].sem, d_dma[gdk])
                                if ic == 1 and d_war:
                                    v.wait_ge(vT.sem, d_war)
                                mm = v.tensor_copy(out=d_all[:, ic * 1024:
                                                           (ic + 1) * 1024],
                                                   in_=d_st[1][:])
                                d_cast[gdk] = ("V", inc(mm, vV))
                                d_marks.append(gdk)
                        uk += 2
                        dck += 1
                        if ic < 20:
                            emit_d_dma()

                    dr_s = max([d_cast[k][1] for k in d_marks[-22:]
                                if d_cast[k][0] == "S"] or [0])
                    dr_v = max([d_cast[k][1] for k in d_marks[-22:]
                                if d_cast[k][0] == "V"] or [0])
                    t_last = unit_t[uk - 1]

                    sub2_evac = {}
                    for gg in range(4):
                        quad = 4 if gg % 2 == 0 else 0

                        @block.tensor
                        def _(t_, e=e, gg=gg, quad=quad, dr_s=dr_s, dr_v=dr_v,
                              t_last=t_last):
                            if gg == 0:
                                if dr_s:
                                    t_.wait_ge(vS.sem, dr_s)
                                if dr_v:
                                    t_.wait_ge(vV.sem, dr_v)
                                t_.wait_ge(vV.sem, t_last)
                            elif gg == 1:
                                t_.wait_ge(vV.sem, t_last)
                            else:
                                t_.wait_ge(vV.sem, sub2_evac[gg - 2])
                            mm = None
                            for ic in range(NC_I):
                                for g2 in range(2):
                                    for dh in range(2):
                                        mm = t_.matmul(
                                            out=pb[quad + 2 * g2 + dh][:],
                                            lhsT=t_full[:, ic * 1024 + gg * 256 +
                                                        g2 * 128:
                                                        ic * 1024 + gg * 256 +
                                                        g2 * 128 + 128],
                                            rhs=d_all[:, ic * 1024 + dh * 512:
                                                      ic * 1024 + dh * 512 + 512],
                                            start=(ic == 0), stop=(ic == NC_I - 1))
                            inc(mm, vT)
                        dpe = vT.n

                        @block.vector
                        def _(v, bt=bt, e=e, gg=gg, quad=quad, dpe=dpe,
                              vflush=vV.n):
                            v.wait_ge(vT.sem, dpe)
                            v.wait_ge(vV.sem, vflush)
                            if e == 0 and gg == 0:
                                v.wait_ge(sYB.sem, y_loaded_b[bt])
                            mm = None
                            for g2 in range(2):
                                for dh in range(2):
                                    gi = gg * 2 + g2
                                    sl = slice(gi * 1024 + dh * 512,
                                               gi * 1024 + dh * 512 + 512)
                                    if e == 7:
                                        mm = v.tensor_tensor(
                                            out=y_b[:, sl], in0=y_b[:, sl],
                                            in1=pb[quad + 2 * g2 + dh][:],
                                            op=ALU.add)
                                    else:
                                        t_a = 2 * bt + gi // 4
                                        g_a = gi % 4
                                        wcol = (t_a * 4 + g_a) * 7 + e
                                        mm = v.scalar_tensor_tensor(
                                            out=y_b[:, sl],
                                            in0=pb[quad + 2 * g2 + dh][:],
                                            scalar=w_sb[:, wcol:wcol + 1],
                                            in1=y_b[:, sl],
                                            op0=ALU.mult, op1=ALU.add)
                            sub2_evac[gg] = inc(mm, vV)
                    t_war = vT.n
                    d_war = vT.n
                    prev_evac3 = max(sub2_evac[1], sub2_evac[3])
                    y_evac_last = vV.n
                    if e == 5 and bt + 1 < NT_B:
                        emit_xn_load_b(bt + 1)
                xn_pe_last[bt] = vT.n

                @block.gpsimd
                def _(g, bt=bt, yy=None):
                    g.wait_ge(vV.sem, y_evac_last)
                    inc(g.dma_start(out=y_out[bt * TT_B:(bt + 1) * TT_B, :]
                                    .rearrange("(j p) d -> p j d", p=128),
                                    in_=y_b[:].rearrange("p (j d) -> p j d", j=8)),
                        sOut, 16)

            if DEBUG:
                @block.sync
                def _(s):
                    s.wait_ge(vV.sem, vV.n)
                    s.wait_ge(vP.sem, vP.n)
                    inc(s.dma_start(out=dbg_t, in_=t_full[:]), sOut, 16)
                    inc(s.dma_start(out=dbg_dl, in_=d_all[:]), sOut, 16)

            @block.gpsimd
            def _(g):
                g.wait_ge(sOut.sem, sOut.n)

    return nc


_BUILD_CACHE = {}


def _get_nc():
    if "nc" not in _BUILD_CACHE:
        _BUILD_CACHE["nc"] = build()
    return _BUILD_CACHE["nc"]


def kernel(L0_attn_x, L0_attn_W_fwd, L0_attn_W_out, L0_ffn_sg, L0_ffn_su,
           L0_ffn_sd, L0_ffn_rg, L0_ffn_ru, L0_ffn_rd, L0_ffn_wr, L0_ffn_rb):
    x = np.ascontiguousarray(np.asarray(L0_attn_x, dtype=np.float32))
    Wf = np.asarray(L0_attn_W_fwd, np.float32)
    Wo = np.asarray(L0_attn_W_out, np.float32)
    sg = np.asarray(L0_ffn_sg, np.float32)
    su = np.asarray(L0_ffn_su, np.float32)
    sd = np.asarray(L0_ffn_sd, np.float32)
    rg = np.asarray(L0_ffn_rg, np.float32)
    ru = np.asarray(L0_ffn_ru, np.float32)
    rd = np.asarray(L0_ffn_rd, np.float32)
    wr = np.asarray(L0_ffn_wr, np.float32)
    rb = np.asarray(L0_ffn_rb, np.float32)

    wf_pk = np.ascontiguousarray(
        Wf.reshape(3, 16, 128, 8, 128).transpose(4, 1, 0, 3, 2).reshape(128, -1))
    wo_pk = np.ascontiguousarray(
        Wo.reshape(2, 512, 16, 128).transpose(3, 2, 0, 1).reshape(128, -1))

    def pack_gu(w):
        return w.reshape(NC_I, 128, 8, 128).transpose(3, 0, 2, 1).reshape(128, -1)

    def pack_d(w):
        return w.T.reshape(NC_I, 128, 1024).transpose(1, 0, 2).reshape(128, -1)

    g_pk = np.ascontiguousarray(
        np.stack([pack_gu(rg[e]) for e in range(7)] + [pack_gu(sg)], 0))
    u_pk = np.ascontiguousarray(
        np.stack([pack_gu(ru[e]) for e in range(7)] + [pack_gu(su)], 0))
    d_pk = np.ascontiguousarray(
        np.stack([pack_d(rd[e]) for e in range(7)] + [pack_d(sd)], 0))
    wr_pk = np.ascontiguousarray(
        wr.T.reshape(8, 128, 7).transpose(1, 0, 2).reshape(128, 56))
    rb_pk = np.ascontiguousarray(rb.reshape(1, 7))
    eye_pk = np.eye(128, dtype=np.float32)

    nc = _get_nc()
    shared = {"wf_p": wf_pk, "wo_p": wo_pk, "g_p": g_pk, "u_p": u_pk,
              "d_p": d_pk, "wr_p": wr_pk, "rb_p": rb_pk, "eye_p": eye_pk}
    in_maps = []
    for c in range(8):
        xc = x[2 * c:2 * c + 2].reshape(TOK, D)
        m = dict(shared)
        m["x_tm"] = np.ascontiguousarray(xc)
        m["x_fm"] = np.ascontiguousarray(xc.T)
        in_maps.append(m)

    res = run_bass_kernel_spmd(nc, in_maps, core_ids=list(range(8)))
    _BUILD_CACHE["res"] = res
    y = np.stack([np.asarray(r["y_out"]) for r in res.results], 0)
    return y.reshape(16, 2048, 1024)


# version 26
# speedup vs baseline: 1.1060x; 1.1060x over previous
import contextlib
import numpy as np
import concourse.bass as bass
import concourse.mybir as mybir
from concourse.bass_utils import run_bass_kernel_spmd

F32 = mybir.dt.float32
BF16 = mybir.dt.bfloat16
AF = mybir.ActivationFunctionType
ALU = mybir.AluOpType
AX = mybir.AxisListType

D = 1024
HID = 2048
I_FF = 2816
TOK = 4096
NT_A = 8
TT_A = 512
NT_B = 4
TT_B = 1024
NC_D = 8
NC_H = 16
NC_I = 22
NEG_INF = -1e30
EPS = 1e-5


class Ctr:
    __slots__ = ("sem", "n")

    def __init__(self, sem):
        self.sem = sem
        self.n = 0


def inc(ins, c, k=1):
    ins.then_inc(c.sem, k)
    c.n += k
    return c.n


DEBUG = False


def build():
    nc = bass.Bass("TRN2", target_bir_lowering=False)

    x_tm = nc.dram_tensor("x_tm", [TOK, D], F32, kind="ExternalInput").ap()
    x_fm = nc.dram_tensor("x_fm", [D, TOK], F32, kind="ExternalInput").ap()
    wf_p = nc.dram_tensor("wf_p", [128, 49152], F32, kind="ExternalInput").ap()
    wo_p = nc.dram_tensor("wo_p", [128, 16384], F32, kind="ExternalInput").ap()
    g_p = nc.dram_tensor("g_p", [8, 128, NC_I * 1024], F32, kind="ExternalInput").ap()
    u_p = nc.dram_tensor("u_p", [8, 128, NC_I * 1024], F32, kind="ExternalInput").ap()
    d_p = nc.dram_tensor("d_p", [8, 128, NC_I * 1024], F32, kind="ExternalInput").ap()
    wr_p = nc.dram_tensor("wr_p", [128, 56], F32, kind="ExternalInput").ap()
    rb_p = nc.dram_tensor("rb_p", [1, 7], F32, kind="ExternalInput").ap()
    eye_p = nc.dram_tensor("eye_p", [128, 128], F32, kind="ExternalInput").ap()
    y_out = nc.dram_tensor("y_out", [TOK, D], F32, kind="ExternalOutput").ap()
    wf_st = nc.dram_tensor("wf_st", [128, 49152], BF16, kind="Internal").ap()
    wo_st = nc.dram_tensor("wo_st", [128, 16384], BF16, kind="Internal").ap()
    xn_st = nc.dram_tensor("xn_st", [128, NT_A * 8 * TT_A], BF16, kind="Internal").ap()
    y1_st = nc.dram_tensor("y1_st", [TOK, D], F32, kind="Internal").ap()
    if DEBUG:
        dbg_xn = nc.dram_tensor("dbg_xn", [128, 32768], BF16, kind="ExternalOutput").ap()
        dbg_y1 = nc.dram_tensor("dbg_y1", [TOK, D], F32, kind="ExternalOutput").ap()
        dbg_w = nc.dram_tensor("dbg_w", [128, 224], F32, kind="ExternalOutput").ap()
        dbg_h = nc.dram_tensor("dbg_h", [128, 16 * 513], BF16, kind="ExternalOutput").ap()
        dbg_t = nc.dram_tensor("dbg_t", [128, NC_I * 1024], BF16, kind="ExternalOutput").ap()

    with contextlib.ExitStack() as top:
        TE = top.enter_context
        sPre = Ctr(TE(nc.semaphore("sPre")))
        sP0L = [Ctr(TE(nc.semaphore(f"sP0L{i}"))) for i in range(2)]
        sP0S = [Ctr(TE(nc.semaphore(f"sP0S{i}"))) for i in range(2)]
        sXi2 = [Ctr(TE(nc.semaphore(f"sXi{i}"))) for i in range(2)]
        sXn = Ctr(TE(nc.semaphore("sXn")))
        sY1 = Ctr(TE(nc.semaphore("sY1")))
        sWfi = [Ctr(TE(nc.semaphore(f"sWfi{i}"))) for i in range(2)]
        sWo = Ctr(TE(nc.semaphore("sWo")))
        sGu = [Ctr(TE(nc.semaphore(f"sGu{i}"))) for i in range(2)]
        sDd = [Ctr(TE(nc.semaphore(f"sDd{i}"))) for i in range(2)]
        sBio = Ctr(TE(nc.semaphore("sBio")))
        sYB = Ctr(TE(nc.semaphore("sYB")))
        sOut = Ctr(TE(nc.semaphore("sOut")))
        vT = Ctr(TE(nc.semaphore("vT")))
        vV = Ctr(TE(nc.semaphore("vV")))
        vS = Ctr(TE(nc.semaphore("vS")))
        vP = Ctr(TE(nc.semaphore("vP")))

        pb = [TE(nc.psum_tensor(f"pb{i}", [128, 512], F32)) for i in range(8)]

        eye_s = TE(nc.sbuf_tensor("eye_s", [128, 128], F32))
        ones_r = TE(nc.sbuf_tensor("ones_r", [1, 128], F32))
        ones_c = TE(nc.sbuf_tensor("ones_c", [128, 1], F32))
        wr_s = TE(nc.sbuf_tensor("wr_s", [128, 56], F32))
        rb_r = TE(nc.sbuf_tensor("rb_r", [1, 7], F32))
        s_row = TE(nc.sbuf_tensor("s_row", [1, 7], F32))
        s_bc = TE(nc.sbuf_tensor("s_bc", [128, 7], F32))
        rb_bc = TE(nc.sbuf_tensor("rb_bc", [128, 7], F32))
        w_sb = TE(nc.sbuf_tensor("w_sb", [128, 224], F32))
        zcol = TE(nc.sbuf_tensor("zcol", [128, 1], F32))
        block = TE(nc.Block())

        @block.gpsimd
        def _(g):
            g.memset(ones_r[:], 1.0)
            g.memset(zcol[:], 0.0)
            mm = g.memset(ones_c[:], 1.0)
            inc(mm, vP)
        const_done = vP.n

        @block.sync
        def _(s):
            inc(s.dma_start(out=eye_s[:], in_=eye_p), sPre, 16)
            inc(s.dma_start(out=wr_s[:], in_=wr_p), sPre, 16)
            inc(s.dma_start(out=rb_r[:], in_=rb_p), sPre, 16)
        pre_dma = sPre.n

        with contextlib.ExitStack() as ph0:
            E0 = ph0.enter_context
            p0_stage = E0(nc.sbuf_tensor("p0_stage", [128, 8192], F32))
            p0_bf = E0(nc.sbuf_tensor("p0_bf", [128, 8192], BF16))
            p0_chunks = [(wf_p, wf_st, i * 4096) for i in range(12)] + \
                        [(wo_p, wo_st, i * 4096) for i in range(4)]
            p0_cast = []
            p0_dma = []
            p0_store = []
            for k, (src, dst, off) in enumerate(p0_chunks):
                o = (k % 2) * 4096

                @block.sync
                def _(s, k=k, o=o, src=src, off=off):
                    if k >= 2:
                        s.wait_ge(vV.sem, p0_cast[k - 2])
                    inc(s.dma_start(out=p0_stage[:, o:o + 4096],
                                    in_=src[:, off:off + 4096]), sP0L[k % 2], 16)
                    p0_dma.append(sP0L[k % 2].n)

                @block.vector
                def _(v, k=k, o=o):
                    v.wait_ge(sP0L[k % 2].sem, p0_dma[k])
                    if k >= 2:
                        v.wait_ge(sP0S[k % 2].sem, p0_store[k - 2])
                    mm = v.tensor_copy(out=p0_bf[:, o:o + 4096],
                                       in_=p0_stage[:, o:o + 4096])
                    p0_cast.append(inc(mm, vV))

                @block.gpsimd
                def _(g, k=k, o=o, dst=dst, off=off):
                    g.wait_ge(vV.sem, p0_cast[k])
                    inc(g.dma_start(out=dst[:, off:off + 4096],
                                    in_=p0_bf[:, o:o + 4096]), sP0S[k % 2], 16)
                    p0_store.append(sP0S[k % 2].n)
            p0_store_all = (sP0S[0].n, sP0S[1].n)

            @block.tensor
            def _(t):
                t.wait_ge(sPre.sem, pre_dma)
                t.wait_ge(vP.sem, const_done)
                for c in range(8):
                    t.matmul(out=pb[6][0:1, 0:7], lhsT=ones_c[:],
                             rhs=wr_s[:, c * 7:c * 7 + 7],
                             start=(c == 0), stop=(c == 7))
                mm = t.matmul(out=pb[7][:, 0:7], lhsT=ones_r[:], rhs=rb_r[0:1, 0:7],
                              start=True, stop=True)
                inc(mm, vT)
            pre_t1 = vT.n

            @block.vector
            def _(v):
                v.wait_ge(vT.sem, pre_t1)
                v.tensor_copy(out=s_row[:], in_=pb[6][0:1, 0:7])
                mm = v.tensor_copy(out=rb_bc[:], in_=pb[7][:, 0:7])
                inc(mm, vV)
            pre_v1 = vV.n

            @block.tensor
            def _(t):
                t.wait_ge(vV.sem, pre_v1)
                mm = t.matmul(out=pb[7][:, 0:7], lhsT=ones_r[:], rhs=s_row[0:1, 0:7],
                              start=True, stop=True)
                inc(mm, vT)
            pre_t2 = vT.n

            @block.vector
            def _(v):
                v.wait_ge(vT.sem, pre_t2)
                mm = v.tensor_copy(out=s_bc[:], in_=pb[7][:, 0:7])
                inc(mm, vV)

        with contextlib.ExitStack() as ph_a:
            E = ph_a.enter_context
            xtm = [E(nc.sbuf_tensor(f"xtm{i}", [128, 4096], F32)) for i in range(2)]
            xfm = [E(nc.sbuf_tensor(f"xfm{i}", [128, 4096], F32)) for i in range(2)]
            xnfm = E(nc.sbuf_tensor("xnfm", [128, 4096], BF16))
            h_bf = E(nc.sbuf_tensor("h_bf", [128, NC_H * 513], BF16))
            y_sb = E(nc.sbuf_tensor("y_sb", [128, 4096], F32))
            wfb = [E(nc.sbuf_tensor(f"wfb{i}", [128, 3072], BF16)) for i in range(2)]
            wob16 = E(nc.sbuf_tensor("wob16", [128, 16 * 512], BF16))
            sq_sc = E(nc.sbuf_tensor("sq_sc", [128, 2048], F32))
            negm = E(nc.sbuf_tensor("negm", [128, 4], F32))
            sums = E(nc.sbuf_tensor("sums", [128, 4], F32))
            rstd = E(nc.sbuf_tensor("rstd", [128, 4], F32))
            tc1 = E(nc.sbuf_tensor("tc1", [128, 4], F32))
            nm_row = E(nc.sbuf_tensor("nm_row", [1, 512], F32))
            rs_row = E(nc.sbuf_tensor("rs_row", [1, 512], F32))
            l8 = E(nc.sbuf_tensor("l8", [128, 32], F32))
            m8 = E(nc.sbuf_tensor("m8", [128, 32], F32))
            nm1 = E(nc.sbuf_tensor("nm1", [128, 4], F32))
            dty = E(nc.sbuf_tensor("dty", [128, 4], F32))
            rdE = E(nc.sbuf_tensor("rdE", [128, 4], F32))
            e28 = E(nc.sbuf_tensor("e28", [128, 28], F32))
            mk28 = E(nc.sbuf_tensor("mk28", [128, 28], F32))
            asc = E(nc.sbuf_tensor("asc", [128, 2048], F32))
            easb = [E(nc.sbuf_tensor(f"easb{i}", [128, 512], F32)) for i in range(2)]
            ebsb = [E(nc.sbuf_tensor(f"ebsb{i}", [128, 512], F32)) for i in range(2)]
            rzsb = E(nc.sbuf_tensor("rzsb", [128, 512], F32))
            ezsb = [E(nc.sbuf_tensor(f"ezsb{i}", [128, 512], F32)) for i in range(2)]
            hrsb = [E(nc.sbuf_tensor(f"hrsb{i}", [128, 512], F32)) for i in range(2)]
            densb = E(nc.sbuf_tensor("densb", [128, 512], F32))
            rdenb = E(nc.sbuf_tensor("rdenb", [128, 512], F32))
            isb = [E(nc.sbuf_tensor(f"isb{i}", [128, 512], F32)) for i in range(2)]
            den2b = E(nc.sbuf_tensor("den2b", [128, 512], F32))
            rd2b = [E(nc.sbuf_tensor(f"rd2b{i}", [128, 512], F32)) for i in range(2)]
            szsb = [E(nc.sbuf_tensor(f"szsb{i}", [128, 512], F32)) for i in range(2)]
            gsb = E(nc.sbuf_tensor("gsb", [128, 512], F32))
            f_bu = [E(nc.sbuf_tensor(f"f_bu{i}", [128, 513], F32)) for i in range(2)]
            v_bu = [E(nc.sbuf_tensor(f"v_bu{i}", [128, 513], F32)) for i in range(2)]

            @block.gpsimd
            def _(g):
                mm = g.memset(l8[:], NEG_INF)
                inc(mm, vP)
                g.memset(f_bu[0][:, 0:1], 0.0)
                mm = g.memset(f_bu[1][:, 0:1], 0.0)
                inc(mm, vP)
            amemset_done = vP.n

            x_loaded = [0] * NT_A
            xtm_free = [0, 0]
            xfm_free = [0, 0]
            xn_ready = [0] * NT_A
            xn_stored = [0] * NT_A
            y1_stored = [0] * NT_A
            wf_dma = {}
            pe_iter = {}
            fg_free = {}
            hid_free = {}
            mark = {}
            hin_w = [0] * 16
            var_done_a = [0] * NT_A
            rstd_done_a = [0] * NT_A
            tp_done_a = [0] * NT_A
            bc_done_a = [0] * NT_A
            rt_done_a = [0] * NT_A
            h_all_a = [0] * NT_A
            wo_pe = {}
            ao_evac = {}

            def emit_xload(t):
                b = t % 2

                @block.gpsimd
                def _(g, t=t, b=b):
                    if t < 2:
                        g.wait_ge(sP0S[0].sem, p0_store_all[0])
                        g.wait_ge(sP0S[1].sem, p0_store_all[1])
                        g.wait_ge(vV.sem, p0_cast[-1])
                    w = max(xtm_free[b], xfm_free[b])
                    if w:
                        g.wait_ge(vV.sem, w)
                    inc(g.dma_start(
                        out=xtm[b][:].rearrange("p (j d) -> p j d", j=4),
                        in_=x_tm[t * TT_A:(t + 1) * TT_A, :]
                        .rearrange("(j p) d -> p j d", p=128)), sXi2[b], 16)
                    inc(g.dma_start(
                        out=xfm[b][:].rearrange("p (c n) -> p c n", c=8),
                        in_=x_fm[:, t * TT_A:(t + 1) * TT_A]
                        .rearrange("(c p) n -> p c n", p=128)), sXi2[b], 16)
                    x_loaded[t] = sXi2[b].n

            def emit_ln(t):
                b = t % 2

                @block.scalar
                def _(s, t=t, b=b):
                    s.wait_ge(sXi2[b].sem, x_loaded[t])
                    mm = None
                    for j in range(4):
                        if j == 2:
                            s.wait_ge(vS.sem, inc(mm, vS))
                        mm = s.activation(out=sq_sc[:, (j % 2) * 1024:
                                              (j % 2 + 1) * 1024],
                                          in_=xtm[b][:, j * 1024:(j + 1) * 1024],
                                          func=AF.Square, accum_out=sums[:, j:j + 1])
                    inc(mm, vS)
                sums_done = vS.n

                @block.vector
                def _(v, t=t, b=b):
                    v.wait_ge(sXi2[b].sem, x_loaded[t])
                    if t > 0:
                        v.wait_ge(vT.sem, tp_done_a[t - 1])
                    mm = None
                    for j in range(4):
                        mm = v.reduce_sum(out=negm[:, j:j + 1],
                                          in_=xtm[b][:, j * 1024:(j + 1) * 1024],
                                          axis=AX.X, negate=True)
                    v.wait_ge(vV.sem, inc(mm, vV))
                    mm = v.tensor_scalar(out=negm[:], in0=negm[:], scalar1=1.0 / D,
                                         scalar2=None, op0=ALU.mult)
                    v.wait_ge(vV.sem, inc(mm, vV))
                    mm = v.tensor_tensor(out=tc1[:], in0=negm[:], in1=negm[:],
                                         op=ALU.mult)
                    v.wait_ge(vV.sem, inc(mm, vV))
                    v.wait_ge(vS.sem, sums_done)
                    mm = v.scalar_tensor_tensor(out=tc1[:], in0=sums[:],
                                                scalar=1.0 / D, in1=tc1[:],
                                                op0=ALU.mult, op1=ALU.subtract)
                    v.wait_ge(vV.sem, inc(mm, vV))
                    mm = v.tensor_scalar(out=tc1[:], in0=tc1[:], scalar1=EPS,
                                         scalar2=None, op0=ALU.add)
                    inc(mm, vV)
                var_done_a[t] = vV.n

                @block.scalar
                def _(s, t=t):
                    s.wait_ge(vV.sem, var_done_a[t])
                    mm = s.activation(out=rstd[:], in_=tc1[:], func=AF.Ln)
                    s.wait_ge(vS.sem, inc(mm, vS))
                    mm = s.activation(out=rstd[:], in_=rstd[:], func=AF.Exp,
                                      scale=-0.5)
                    inc(mm, vS)
                rstd_done_a[t] = vS.n

            def emit_tpbc(t):
                @block.tensor
                def _(t_, t=t):
                    t_.wait_ge(vS.sem, rstd_done_a[t])
                    t_.wait_ge(vV.sem, var_done_a[t])
                    for j in range(4):
                        t_.transpose(out=pb[6][0:1, j * 128:(j + 1) * 128],
                                     in_=negm[:, j:j + 1], identity=eye_s[:])
                    for j in range(4):
                        mm = t_.transpose(out=pb[7][0:1, j * 128:(j + 1) * 128],
                                          in_=rstd[:, j:j + 1], identity=eye_s[:])
                    inc(mm, vT)
                tp_done_a[t] = vT.n

                @block.vector
                def _(v, t=t):
                    v.wait_ge(vT.sem, tp_done_a[t])
                    v.tensor_copy(out=nm_row[:], in_=pb[6][0:1, :])
                    mm = v.tensor_copy(out=rs_row[:], in_=pb[7][0:1, :])
                    inc(mm, vV)
                rows_done = vV.n

                @block.tensor
                def _(t_, t=t, rows_done=rows_done):
                    t_.wait_ge(vV.sem, rows_done)
                    t_.matmul(out=pb[6][:], lhsT=ones_r[:], rhs=nm_row[:],
                              start=True, stop=True)
                    mm = t_.matmul(out=pb[7][:], lhsT=ones_r[:], rhs=rs_row[:],
                                   start=True, stop=True)
                    inc(mm, vT)
                bc_done_a[t] = vT.n

            def emit_xn(t):
                b = t % 2

                @block.vector
                def _(v, t=t, b=b):
                    v.wait_ge(vT.sem, bc_done_a[t])
                    if t > 0:
                        v.wait_ge(sXn.sem, xn_stored[t - 1])
                    mm = None
                    for grp in range(2):
                        for c in range(grp * 4, grp * 4 + 4):
                            sl = slice(c * TT_A, (c + 1) * TT_A)
                            a_sl = slice((c % 4) * TT_A, (c % 4 + 1) * TT_A)
                            mm = v.tensor_tensor(out=asc[:, a_sl],
                                                 in0=xfm[b][:, sl], in1=pb[6][:],
                                                 op=ALU.add)
                        v.wait_ge(vV.sem, inc(mm, vV))
                        for c in range(grp * 4, grp * 4 + 4):
                            sl = slice(c * TT_A, (c + 1) * TT_A)
                            a_sl = slice((c % 4) * TT_A, (c % 4 + 1) * TT_A)
                            mm = v.tensor_tensor(out=xnfm[:, sl], in0=asc[:, a_sl],
                                                 in1=pb[7][:], op=ALU.mult)
                        if grp == 0:
                            v.wait_ge(vV.sem, inc(mm, vV))
                    inc(mm, vV)
                xn_ready[t] = vV.n

                @block.gpsimd
                def _(g, t=t):
                    g.wait_ge(vV.sem, xn_ready[t])
                    inc(g.dma_start(out=xn_st[:, t * 4096:(t + 1) * 4096],
                                    in_=xnfm[:]), sXn, 16)
                    xn_stored[t] = sXn.n

            def emit_router(t):
                b = t % 2

                @block.tensor
                def _(t_, t=t, b=b):
                    t_.wait_ge(vV.sem, xn_ready[t])
                    mm = None
                    for g in range(4):
                        for c in range(NC_D):
                            mm = t_.matmul(
                                out=pb[6][:, g * 8:g * 8 + 7],
                                lhsT=xfm[b][:, c * TT_A + g * 128:
                                            c * TT_A + g * 128 + 128],
                                rhs=wr_s[:, c * 7:c * 7 + 7],
                                start=(c == 0), stop=(c == NC_D - 1))
                    inc(mm, vT)
                rt_done_a[t] = vT.n

                @block.vector
                def _(v, t=t):
                    v.wait_ge(vT.sem, rt_done_a[t])
                    v.wait_ge(vP.sem, amemset_done)
                    mm = None
                    for g in range(4):
                        mm = v.scalar_tensor_tensor(out=l8[:, g * 8:g * 8 + 7],
                                                    in0=s_bc[:],
                                                    scalar=negm[:, g:g + 1],
                                                    in1=pb[6][:, g * 8:g * 8 + 7],
                                                    op0=ALU.mult, op1=ALU.add)
                    v.wait_ge(vV.sem, inc(mm, vV))
                    for g in range(4):
                        mm = v.tensor_single_scalar(out=l8[:, g * 8:g * 8 + 7],
                                                    in_=l8[:, g * 8:g * 8 + 7],
                                                    scalar=rstd[:, g:g + 1],
                                                    op=ALU.mult)
                    v.wait_ge(vV.sem, inc(mm, vV))
                    for g in range(4):
                        mm = v.tensor_tensor(out=l8[:, g * 8:g * 8 + 7],
                                             in0=l8[:, g * 8:g * 8 + 7],
                                             in1=rb_bc[:], op=ALU.add)
                    v.wait_ge(vV.sem, inc(mm, vV))
                    for g in range(4):
                        mm = v.max(out=m8[:, g * 8:g * 8 + 8],
                                   in_=l8[:, g * 8:g * 8 + 8])
                    v.wait_ge(vV.sem, inc(mm, vV))
                    for g in range(4):
                        ls = slice(g * 8, g * 8 + 7)
                        v.tensor_scalar(out=nm1[:, g:g + 1],
                                        in0=m8[:, g * 8:g * 8 + 1],
                                        scalar1=-1.0, scalar2=None, op0=ALU.mult)
                        v.tensor_tensor(out=dty[:, g:g + 1],
                                        in0=m8[:, g * 8 + 1:g * 8 + 2],
                                        in1=m8[:, g * 8:g * 8 + 1], op=ALU.subtract)
                        mm = v.tensor_single_scalar(out=mk28[:, g * 7:g * 7 + 7],
                                                    in_=l8[:, ls],
                                                    scalar=m8[:, g * 8 + 1:g * 8 + 2],
                                                    op=ALU.is_ge)
                    inc(mm, vV)
                rt_v = vV.n

                @block.scalar
                def _(s, t=t, rt_v=rt_v):
                    s.wait_ge(vV.sem, rt_v)
                    for g in range(4):
                        s.activation(out=e28[:, g * 7:g * 7 + 7],
                                     in_=l8[:, g * 8:g * 8 + 7], func=AF.Exp,
                                     bias=nm1[:, g:g + 1])
                    mm = s.activation(out=rdE[:], in_=dty[:], func=AF.Exp)
                    inc(mm, vS)
                rt_s = vS.n

                @block.vector
                def _(v, t=t, rt_s=rt_s):
                    v.wait_ge(vS.sem, rt_s)
                    mm = v.tensor_scalar(out=rdE[:], in0=rdE[:], scalar1=1.0,
                                         scalar2=None, op0=ALU.add)
                    v.wait_ge(vV.sem, inc(mm, vV))
                    mm = v.reciprocal(out=rdE[:], in_=rdE[:])
                    v.wait_ge(vV.sem, inc(mm, vV))
                    mm = None
                    for g in range(4):
                        mm = v.tensor_single_scalar(out=e28[:, g * 7:g * 7 + 7],
                                                    in_=e28[:, g * 7:g * 7 + 7],
                                                    scalar=rdE[:, g:g + 1],
                                                    op=ALU.mult)
                    v.wait_ge(vV.sem, inc(mm, vV))
                    mm = None
                    for g in range(4):
                        es = slice(g * 7, g * 7 + 7)
                        wc = (t * 4 + g) * 7
                        mm = v.tensor_tensor(out=w_sb[:, wc:wc + 7], in0=e28[:, es],
                                             in1=mk28[:, es], op=ALU.mult)
                    inc(mm, vV)
                xfm_free[b] = vV.n

            def emit_gates_iter(t, hp):
                idx = t * 16 + hp
                wbi = idx % 2
                trio = 3 * (idx % 2)

                @block.sync
                def _(s, idx=idx, wbi=wbi, hp=hp):
                    if idx == 0:
                        s.wait_ge(sP0S[0].sem, p0_store_all[0])
                        s.wait_ge(sP0S[1].sem, p0_store_all[1])
                    if idx >= 2:
                        s.wait_ge(vT.sem, pe_iter[idx - 2][2])
                    inc(s.dma_start(out=wfb[wbi][:],
                                    in_=wf_st[:, hp * 3072:hp * 3072 + 3072]),
                        sWfi[wbi], 16)
                    wf_dma[idx] = sWfi[wbi].n

                @block.tensor
                def _(t_, t=t, hp=hp, idx=idx, wbi=wbi, trio=trio):
                    t_.wait_ge(sWfi[wbi].sem, wf_dma[idx])
                    if hp == 0:
                        t_.wait_ge(vV.sem, xn_ready[t])
                    if hp <= 1 and t > 0:
                        t_.wait_ge(vV.sem, ao_evac[(t - 1, 1)])
                    if idx >= 2:
                        t_.wait_ge(vS.sem, fg_free[idx - 2])
                        t_.wait_ge(vV.sem, hid_free[idx - 2])
                        t_.wait_ge(vS.sem, mark[(idx - 2, "rd2")])
                    marks = []
                    for q in range(3):
                        mm = None
                        for c in range(NC_D):
                            mm = t_.matmul(
                                out=pb[trio + q][:],
                                lhsT=wfb[wbi][:, (q * 8 + c) * 128:
                                              (q * 8 + c) * 128 + 128],
                                rhs=xnfm[:, c * TT_A:(c + 1) * TT_A],
                                start=(c == 0), stop=(c == NC_D - 1))
                        marks.append(inc(mm, vT))
                    pe_iter[idx] = tuple(marks)

                q0, q1v, q2 = pe_iter[idx]

                @block.scalar
                def _(s, t=t, hp=hp, idx=idx, trio=trio, q0=q0, q1v=q1v, q2=q2):
                    par = idx % 2
                    if idx > 1:
                        s.wait_ge(vV.sem, mark[(idx - 2, "sgz")])
                        s.wait_ge(vP.sem, mark[(idx - 2, "g")])
                    s.wait_ge(vT.sem, q0)
                    s.activation(out=easb[par][:], in_=pb[trio][:], func=AF.Exp,
                                 scale=-1.0)
                    s.wait_ge(vT.sem, q1v)
                    mm = s.activation(out=ebsb[par][:], in_=pb[trio + 1][:],
                                      func=AF.Exp, scale=-1.0)
                    fg_free[idx] = inc(mm, vS)
                    s.wait_ge(vT.sem, q2)
                    mm = s.activation(out=rzsb[:], in_=pb[trio + 2][:], func=AF.Relu,
                                      scale=-1.0)
                    s.wait_ge(vS.sem, inc(mm, vS))
                    s.activation(out=ezsb[par][:], in_=rzsb[:], func=AF.Exp,
                                 scale=-1.0)
                    s.activation(out=den2b[:], in_=ezsb[par][:], func=AF.Ln,
                                 bias=1.0)
                    mm = s.activation(out=rd2b[par][:], in_=den2b[:], func=AF.Exp,
                                      scale=-1.0)
                    mark[(idx, "rd2")] = inc(mm, vS)
                    if idx > 1:
                        s.wait_ge(vV.sem, mark[(idx - 2, "scan")])
                    if t == 0 or t == 4:
                        s.wait_ge(vP.sem, const_done)
                        mm = s.copy(out=v_bu[idx % 2][:, 0:1], in_=zcol[:])
                    else:
                        if hin_w[hp]:
                            s.wait_ge(vV.sem, hin_w[hp])
                        mm = s.copy(out=v_bu[idx % 2][:, 0:1],
                                    in_=h_bf[:, hp * 513 + 512:hp * 513 + 513])
                    mark[(idx, "carry")] = inc(mm, vS)

                @block.vector
                def _(v, idx=idx, trio=trio, q2=q2):
                    par = idx % 2
                    if idx > 1:
                        v.wait_ge(vP.sem, mark[(idx - 2, "v")])
                    v.wait_ge(vS.sem, fg_free[idx])
                    mm = v.scalar_tensor_tensor(out=densb[:], in0=easb[par][:],
                                                scalar=2.0, in1=ebsb[par][:],
                                                op0=ALU.add, op1=ALU.add)
                    v.wait_ge(vV.sem, inc(mm, vV))
                    mm = v.reciprocal(out=rdenb[:], in_=densb[:])
                    v.wait_ge(vV.sem, inc(mm, vV))
                    mm = v.scalar_tensor_tensor(out=f_bu[par][:, 1:513],
                                                in0=ebsb[par][:],
                                                scalar=1.0, in1=rdenb[:],
                                                op0=ALU.add, op1=ALU.mult)
                    mark[(idx, "f")] = inc(mm, vV)
                    mm = v.scalar_tensor_tensor(out=isb[par][:], in0=easb[par][:],
                                                scalar=1.0, in1=rdenb[:],
                                                op0=ALU.add, op1=ALU.mult)
                    mark[(idx, "i")] = inc(mm, vV)
                    v.wait_ge(vT.sem, q2)
                    mm = v.tensor_scalar(out=hrsb[par][:], in0=pb[trio + 2][:],
                                         scalar1=0.0, scalar2=None, op0=ALU.max)
                    hid_free[idx] = inc(mm, vV)
                    mark[(idx, "sgz")] = vV.n

                @block.gpsimd
                def _(g, t=t, hp=hp, idx=idx):
                    par = idx % 2
                    g.wait_ge(vV.sem, hid_free[idx])
                    g.wait_ge(vS.sem, mark[(idx, "rd2")])
                    mm = g.tensor_tensor(out=szsb[par][:], in0=ezsb[par][:],
                                         in1=rd2b[par][:], op=ALU.mult)
                    cnt = inc(mm, vP)
                    g.wait_ge(vP.sem, cnt)
                    mm = g.tensor_tensor(out=gsb[:], in0=szsb[par][:],
                                         in1=hrsb[par][:], op=ALU.add)
                    mark[(idx, "g")] = inc(mm, vP)
                    g.wait_ge(vP.sem, mark[(idx, "g")])
                    if idx > 1:
                        g.wait_ge(vV.sem, mark[(idx - 2, "scan")])
                    mm = g.tensor_tensor(out=v_bu[par][:, 1:513], in0=isb[par][:],
                                         in1=gsb[:], op=ALU.mult)
                    mark[(idx, "v")] = inc(mm, vP)

                def emit_scan(j, t=t):
                    jp = j % 16

                    @block.vector
                    def _(v, j=j, jp=jp):
                        v.wait_ge(vP.sem, mark[(j, "v")])
                        v.wait_ge(vS.sem, mark[(j, "carry")])
                        mm = v.tensor_tensor_scan(out=h_bf[:, jp * 513:
                                                         (jp + 1) * 513],
                                                  data0=f_bu[j % 2][:],
                                                  data1=v_bu[j % 2][:], initial=0.0,
                                                  op0=ALU.mult, op1=ALU.add)
                        mark[(j, "scan")] = inc(mm, vV)
                        hin_w[jp] = mark[(j, "scan")]

                if hp > 0:
                    emit_scan(idx - 1)
                if hp == 15:
                    emit_scan(idx)
                    h_all_a[t] = vV.n

            def emit_attn_out(t, half):
                b = t % 2

                @block.sync
                def _(s, t=t, half=half):
                    if half == 0:
                        if t > 0:
                            s.wait_ge(vT.sem, wo_pe[(t - 1, 1)])
                        else:
                            s.wait_ge(sP0S[0].sem, p0_store_all[0])
                            s.wait_ge(sP0S[1].sem, p0_store_all[1])
                    else:
                        s.wait_ge(vT.sem, wo_pe[(t, 0)])
                    for hc in range(NC_H):
                        inc(s.dma_start(
                            out=wob16[:, hc * 512:(hc + 1) * 512],
                            in_=wo_st[:, (hc * 2 + half) * 512:
                                      (hc * 2 + half) * 512 + 512]), sWo, 16)
                    wf_dma[(t, half, "wo")] = sWo.n

                @block.tensor
                def _(t_, t=t, half=half):
                    t_.wait_ge(sWo.sem, wf_dma[(t, half, "wo")])
                    if half == 0:
                        t_.wait_ge(vS.sem, mark[(t * 16 + 15, "rd2")])
                        t_.wait_ge(vV.sem, hid_free[t * 16 + 15])
                        if t > 0:
                            t_.wait_ge(vV.sem, ao_evac[(t - 1, 1)])
                    else:
                        t_.wait_ge(vV.sem, ao_evac[(t, 0)])
                    mm = None
                    for hc in range(NC_H):
                        if half == 0:
                            t_.wait_ge(vV.sem, mark[(t * 16 + hc, "scan")])
                        for g in range(4):
                            mm = t_.matmul(
                                out=pb[g][:],
                                lhsT=h_bf[:, hc * 513 + 1 + g * 128:
                                          hc * 513 + 1 + g * 128 + 128],
                                rhs=wob16[:, hc * 512:(hc + 1) * 512],
                                start=(hc == 0), stop=(hc == NC_H - 1))
                    inc(mm, vT)
                    wo_pe[(t, half)] = vT.n
                ao_pe = vT.n

                @block.vector
                def _(v, t=t, b=b, half=half, ao_pe=ao_pe):
                    v.wait_ge(vT.sem, ao_pe)
                    if t > 0 and half == 0:
                        v.wait_ge(sY1.sem, y1_stored[t - 1])
                    mm = None
                    for g in range(4):
                        sl = slice(g * 1024 + half * 512,
                                   g * 1024 + half * 512 + 512)
                        mm = v.tensor_tensor(out=y_sb[:, sl], in0=xtm[b][:, sl],
                                             in1=pb[g][:], op=ALU.add)
                    ao_evac[(t, half)] = inc(mm, vV)
                if half == 1:
                    xtm_free[b] = vV.n

            def emit_ystore(t):
                @block.gpsimd
                def _(g, t=t):
                    g.wait_ge(vV.sem, ao_evac[(t, 0)])
                    g.wait_ge(vV.sem, ao_evac[(t, 1)])
                    inc(g.dma_start(out=y1_st[t * TT_A:(t + 1) * TT_A, :]
                                    .rearrange("(j p) d -> p j d", p=128),
                                    in_=y_sb[:].rearrange("p (j d) -> p j d", j=4)),
                        sY1, 16)
                    y1_stored[t] = sY1.n

            emit_xload(0)
            emit_xload(1)
            emit_ln(0)
            emit_tpbc(0)
            emit_xn(0)
            emit_router(0)

            for t in range(NT_A):
                for hp in range(16):
                    emit_gates_iter(t, hp)
                if t + 1 < NT_A:
                    emit_ln(t + 1)
                emit_attn_out(t, 0)
                if t + 1 < NT_A:
                    emit_tpbc(t + 1)
                    emit_xn(t + 1)
                emit_attn_out(t, 1)
                if t + 2 < NT_A:
                    emit_xload(t + 2)
                if t + 1 < NT_A:
                    emit_router(t + 1)
                emit_ystore(t)

            if DEBUG:
                @block.sync
                def _(s):
                    s.wait_ge(vS.sem, vS.n)
                    s.wait_ge(vV.sem, vV.n)
                    s.wait_ge(vP.sem, vP.n)
                    s.wait_ge(sXn.sem, sXn.n)
                    s.wait_ge(sY1.sem, sY1.n)
                    inc(s.dma_start(out=dbg_h, in_=h_bf[:]), sOut, 16)
                    inc(s.dma_start(out=dbg_w, in_=w_sb[:]), sOut, 16)
                    inc(s.dma_start(out=dbg_xn, in_=xn_st), sOut, 16)
                    inc(s.dma_start(out=dbg_y1, in_=y1_st), sOut, 16)

            a_V, a_S, a_P, a_T = vV.n, vS.n, vP.n, vT.n
            a_dma = [(sXn, sXn.n), (sY1, sY1.n), (sWo, sWo.n),
                     (sWfi[0], sWfi[0].n), (sWfi[1], sWfi[1].n),
                     (sXi2[0], sXi2[0].n), (sXi2[1], sXi2[1].n),
                     (sP0S[0], sP0S[0].n), (sP0S[1], sP0S[1].n)]

        for eng in ("vector", "scalar", "gpsimd", "tensor", "sync"):
            @getattr(block, eng)
            def _(e):
                e.wait_ge(vV.sem, a_V)
                e.wait_ge(vS.sem, a_S)
                e.wait_ge(vP.sem, a_P)
                e.wait_ge(vT.sem, a_T)
                for c_, n_ in a_dma:
                    e.wait_ge(c_.sem, n_)

        with contextlib.ExitStack() as ph_b:
            E = ph_b.enter_context
            xn_b = [E(nc.sbuf_tensor(f"xn_b{i}", [128, 8 * TT_B], BF16)) for i in range(2)]
            y_b = E(nc.sbuf_tensor("y_b", [128, 8 * 1024], F32))
            t_full = E(nc.sbuf_tensor("t_full", [128, NC_I * 1024], BF16))
            d_all = E(nc.sbuf_tensor("d_all", [128, NC_I * 1024], BF16))
            g_st = [E(nc.sbuf_tensor(f"g_st{i}", [128, 1024], F32)) for i in range(2)]
            u_st = [E(nc.sbuf_tensor(f"u_st{i}", [128, 1024], F32)) for i in range(2)]
            g_bf = [E(nc.sbuf_tensor(f"g_bf{i}", [128, 1024], BF16)) for i in range(3)]
            u_bf = [E(nc.sbuf_tensor(f"u_bf{i}", [128, 1024], BF16)) for i in range(3)]
            d_st = [E(nc.sbuf_tensor(f"d_st{i}", [128, 2048], F32)) for i in range(2)]
            s_sc = [E(nc.sbuf_tensor(f"s_sc{i}", [128, 512], F32)) for i in range(2)]
            sGu4 = [Ctr(TE(nc.semaphore(f"sGu4_{i}"))) for i in range(2)]

            sout_base = sOut.n
            io_done = [0] * NT_B
            gu_dma = {}
            gu_cast = {}
            gu_pe_last = {}
            unit_pe = {}
            unit_silu = {}
            unit_t = {}
            d_dma = {}
            d_cast = {}
            d_marks = []
            ddk = [0]
            t_war = 0
            d_war = 0
            prev_evac3 = 0
            xn_pe_last = [0] * NT_B
            uk = 0
            ick = 0
            dck = 0
            ic_total = 0

            def emit_gu_fetch(ick2, ic2, gsrc, usrc):
                ssl = ick2 % 2
                sbi = ick2 % 3

                @block.sync
                def _(s):
                    inc(s.dma_start(out=g_st[ssl][:],
                                    in_=gsrc[:, ic2 * 1024:(ic2 + 1) * 1024]),
                        sGu4[ssl], 16)
                    inc(s.dma_start(out=u_st[ssl][:],
                                    in_=usrc[:, ic2 * 1024:(ic2 + 1) * 1024]),
                        sGu4[ssl], 16)
                    gu_dma[ick2] = sGu4[ssl].n

                @block.scalar
                def _(s):
                    s.wait_ge(sGu4[ssl].sem, gu_dma[ick2])
                    s.copy(out=g_bf[sbi][:], in_=g_st[ssl][:])
                    mm = s.copy(out=u_bf[sbi][:], in_=u_st[ssl][:])
                    gu_cast[ick2] = inc(mm, vS)

            xn_loaded_b = [0] * (NT_B + 1)
            y_loaded_b = [0] * NT_B

            def emit_xn_load_b(bt):
                @block.gpsimd
                def _(g, bt=bt):
                    if bt >= 2:
                        g.wait_ge(vT.sem, xn_pe_last[bt - 2])
                    for u in range(2):
                        inc(g.dma_start(
                            out=bass.AP(xn_b[bt % 2], u * 512,
                                        [[8192, 128], [1024, 8], [1, 512]]),
                            in_=xn_st[:, (2 * bt + u) * 4096:(2 * bt + u + 1) * 4096]
                            .rearrange("p (c s) -> p c s", c=8)), sBio, 16)
                    xn_loaded_b[bt] = sBio.n

            def emit_y_load_b(bt):
                @block.gpsimd
                def _(g, bt=bt):
                    if bt > 0:
                        g.wait_ge(sOut.sem, sout_base + 16 * bt)
                        g.wait_ge(vV.sem, y_evac_last)
                    inc(g.dma_start(
                        out=y_b[:].rearrange("p (j d) -> p j d", j=8),
                        in_=y1_st[bt * TT_B:(bt + 1) * TT_B, :]
                        .rearrange("(j p) d -> p j d", p=128)), sYB, 16)
                    y_loaded_b[bt] = sYB.n

            emit_xn_load_b(0)
            for bt in range(NT_B):
                emit_y_load_b(bt)
                io_done[bt] = 0

                srcs = [(g_p[e], u_p[e], d_p[e]) for e in range(8)]

                def glb(e_, ic_):
                    return bt * 8 * NC_I + e_ * NC_I + ic_

                def src_of(k):
                    e_, ic_ = divmod(k % (8 * NC_I), NC_I)
                    return srcs[e_][0], srcs[e_][1], ic_

                if bt == 0:
                    for k in range(2):
                        gs_, us_, ic_ = src_of(k)
                        emit_gu_fetch(k, ic_, gs_, us_)

                for e in range(8):
                    gsrc, usrc, dsrc = srcs[e]

                    def emit_d_dma(dsrc=dsrc, e=e):
                        gdk = ddk[0]
                        dk = gdk % 11
                        dbi = gdk % 2

                        @block.gpsimd
                        def _(g, gdk=gdk, dk=dk, dbi=dbi, dsrc=dsrc):
                            w = d_cast.get(gdk - 2, 0)
                            if w:
                                g.wait_ge(vS.sem, w)
                            inc(g.dma_start(out=d_st[dbi][:],
                                            in_=dsrc[:, dk * 2048:(dk + 1) * 2048]),
                                sDd[dbi], 16)
                            d_dma[gdk] = sDd[dbi].n
                        ddk[0] += 1

                    emit_d_dma()
                    emit_d_dma()

                    for ic in range(NC_I):
                        k2 = glb(e, ic) + 2
                        has_fetch = k2 < 4 * 8 * NC_I
                        if has_fetch:
                            gs_, us_, ic2 = src_of(k2)
                            sb2 = k2 % 3
                            ss2 = k2 % 2

                            @block.sync
                            def _(s, k2=k2, ss2=ss2, ic2=ic2, gs_=gs_, us_=us_):
                                w = gu_cast.get(k2 - 2, 0)
                                if w:
                                    s.wait_ge(vS.sem, w)
                                inc(s.dma_start(out=g_st[ss2][:],
                                                in_=gs_[:, ic2 * 1024:(ic2 + 1) * 1024]),
                                    sGu4[ss2], 16)
                                inc(s.dma_start(out=u_st[ss2][:],
                                                in_=us_[:, ic2 * 1024:(ic2 + 1) * 1024]),
                                    sGu4[ss2], 16)
                                gu_dma[k2] = sGu4[ss2].n

                        ickg = glb(e, ic)
                        sbi = ickg % 3
                        uk0, uk1 = uk, uk + 1

                        for half in range(2):
                            bx = 2 * ((uk0 + half) % 2)
                            tok0 = half * 512

                            @block.tensor
                            def _(t_, bt=bt, e=e, ic=ic, ickg=ickg, uk=uk0 + half,
                                  sbi=sbi, half=half, bx=bx, tok0=tok0,
                                  pe3=prev_evac3):
                                if half == 0:
                                    t_.wait_ge(vS.sem, gu_cast[ickg])
                                if ic == 0 and half == 0:
                                    t_.wait_ge(sBio.sem, xn_loaded_b[bt])
                                if ic == 0 and pe3:
                                    t_.wait_ge(vV.sem, pe3)
                                w_s = unit_silu.get(uk - 2, 0)
                                w_t = unit_t.get(uk - 2, 0)
                                if w_s:
                                    t_.wait_ge(vS.sem, w_s)
                                if w_t:
                                    t_.wait_ge(vV.sem, w_t)
                                for c in range(NC_D):
                                    t_.matmul(out=pb[bx][:],
                                              lhsT=g_bf[sbi][:, c * 128:(c + 1) * 128],
                                              rhs=xn_b[bt % 2][:, c * TT_B + tok0:
                                                     c * TT_B + tok0 + 512],
                                              start=(c == 0), stop=(c == NC_D - 1))
                                mm = None
                                for c in range(NC_D):
                                    mm = t_.matmul(out=pb[bx + 1][:],
                                                   lhsT=u_bf[sbi][:,
                                                              c * 128:(c + 1) * 128],
                                                   rhs=xn_b[bt % 2][:,
                                                            c * TT_B + tok0:
                                                            c * TT_B + tok0 + 512],
                                                   start=(c == 0),
                                                   stop=(c == NC_D - 1))
                                unit_pe[uk] = inc(mm, vT)
                            gu_pe_last[ickg] = vT.n

                        bx0 = 2 * (uk0 % 2)
                        bx1 = 2 * (uk1 % 2)

                        @block.scalar
                        def _(s, bt=bt, e=e, ic=ic, uk0=uk0, uk1=uk1, bx0=bx0,
                              bx1=bx1, has_fetch=has_fetch,
                              k2=k2, dsrc=dsrc, dck=dck):
                            if has_fetch:
                                sb2 = k2 % 3
                                ss2 = k2 % 2
                                w = gu_pe_last.get(k2 - 3, 0)
                                if w:
                                    s.wait_ge(vT.sem, w)
                                s.wait_ge(sGu4[ss2].sem, gu_dma[k2])
                                s.copy(out=g_bf[sb2][:], in_=g_st[ss2][:])
                                mm = s.copy(out=u_bf[sb2][:], in_=u_st[ss2][:])
                                gu_cast[k2] = inc(mm, vS)
                            if ic % 2 == 0 and ic // 2 < 11:
                                dk = ic // 2
                                gdk = dck
                                dbi = gdk % 2
                                s.wait_ge(sDd[dbi].sem, d_dma[gdk])
                                if dk == 0 and d_war:
                                    s.wait_ge(vT.sem, d_war)
                                mm = s.copy(out=d_all[:, dk * 2048:(dk + 1) * 2048],
                                            in_=d_st[dbi][:])
                                d_cast[gdk] = inc(mm, vS)
                                d_marks.append(gdk)
                            w = unit_t.get(uk0 - 2, 0)
                            if w:
                                s.wait_ge(vV.sem, w)
                            s.wait_ge(vT.sem, unit_pe[uk0])
                            mm = s.activation(out=s_sc[0][:], in_=pb[bx0][:],
                                              func=AF.Silu)
                            unit_silu[uk0] = inc(mm, vS)
                            w = unit_t.get(uk1 - 2, 0)
                            if w:
                                s.wait_ge(vV.sem, w)
                            s.wait_ge(vT.sem, unit_pe[uk1])
                            mm = s.activation(out=s_sc[1][:], in_=pb[bx1][:],
                                              func=AF.Silu)
                            unit_silu[uk1] = inc(mm, vS)
                        if ic % 2 == 0 and ic // 2 < 11:
                            dck += 1
                            if ic // 2 < 9:
                                emit_d_dma()

                        @block.vector
                        def _(v, ic=ic, uk0=uk0, uk1=uk1, bx0=bx0, bx1=bx1,
                              t_war=t_war):
                            v.wait_ge(vS.sem, unit_silu[uk0])
                            if ic == 0 and t_war:
                                v.wait_ge(vT.sem, t_war)
                            mm = v.tensor_tensor(
                                out=t_full[:, ic * 1024:ic * 1024 + 512],
                                in0=s_sc[0][:], in1=pb[bx0 + 1][:], op=ALU.mult)
                            unit_t[uk0] = inc(mm, vV)
                            v.wait_ge(vS.sem, unit_silu[uk1])
                            mm = v.tensor_tensor(
                                out=t_full[:, ic * 1024 + 512:ic * 1024 + 1024],
                                in0=s_sc[1][:], in1=pb[bx1 + 1][:], op=ALU.mult)
                            unit_t[uk1] = inc(mm, vV)
                        uk += 2

                    d_ready = max(d_cast.get(k, 0) for k in d_marks[-11:]) \
                        if d_marks else 0
                    t_last = unit_t[uk - 1]

                    sub2_evac = {}
                    for gg in range(4):
                        quad = 4 if gg % 2 == 0 else 0

                        @block.tensor
                        def _(t_, e=e, gg=gg, quad=quad, d_ready=d_ready,
                              t_last=t_last):
                            if gg == 0:
                                if d_ready:
                                    t_.wait_ge(vS.sem, d_ready)
                                t_.wait_ge(vV.sem, t_last)
                            elif gg == 1:
                                t_.wait_ge(vV.sem, t_last)
                            else:
                                t_.wait_ge(vV.sem, sub2_evac[gg - 2])
                            mm = None
                            for ic in range(NC_I):
                                for g2 in range(2):
                                    for dh in range(2):
                                        mm = t_.matmul(
                                            out=pb[quad + 2 * g2 + dh][:],
                                            lhsT=t_full[:, ic * 1024 + gg * 256 +
                                                        g2 * 128:
                                                        ic * 1024 + gg * 256 +
                                                        g2 * 128 + 128],
                                            rhs=d_all[:, ic * 1024 + dh * 512:
                                                      ic * 1024 + dh * 512 + 512],
                                            start=(ic == 0), stop=(ic == NC_I - 1))
                            inc(mm, vT)
                        dpe = vT.n

                        @block.vector
                        def _(v, bt=bt, e=e, gg=gg, quad=quad, dpe=dpe,
                              vflush=vV.n):
                            v.wait_ge(vT.sem, dpe)
                            v.wait_ge(vV.sem, vflush)
                            if e == 0 and gg == 0:
                                v.wait_ge(sYB.sem, y_loaded_b[bt])
                            mm = None
                            for g2 in range(2):
                                for dh in range(2):
                                    gi = gg * 2 + g2
                                    sl = slice(gi * 1024 + dh * 512,
                                               gi * 1024 + dh * 512 + 512)
                                    if e == 7:
                                        mm = v.tensor_tensor(
                                            out=y_b[:, sl], in0=y_b[:, sl],
                                            in1=pb[quad + 2 * g2 + dh][:],
                                            op=ALU.add)
                                    else:
                                        t_a = 2 * bt + gi // 4
                                        g_a = gi % 4
                                        wcol = (t_a * 4 + g_a) * 7 + e
                                        mm = v.scalar_tensor_tensor(
                                            out=y_b[:, sl],
                                            in0=pb[quad + 2 * g2 + dh][:],
                                            scalar=w_sb[:, wcol:wcol + 1],
                                            in1=y_b[:, sl],
                                            op0=ALU.mult, op1=ALU.add)
                            sub2_evac[gg] = inc(mm, vV)
                    t_war = vT.n
                    d_war = vT.n
                    prev_evac3 = max(sub2_evac[1], sub2_evac[3])
                    y_evac_last = vV.n
                    if e == 5 and bt + 1 < NT_B:
                        emit_xn_load_b(bt + 1)
                xn_pe_last[bt] = vT.n

                @block.gpsimd
                def _(g, bt=bt, yy=None):
                    g.wait_ge(vV.sem, y_evac_last)
                    inc(g.dma_start(out=y_out[bt * TT_B:(bt + 1) * TT_B, :]
                                    .rearrange("(j p) d -> p j d", p=128),
                                    in_=y_b[:].rearrange("p (j d) -> p j d", j=8)),
                        sOut, 16)

            if DEBUG:
                @block.sync
                def _(s):
                    s.wait_ge(vV.sem, vV.n)
                    inc(s.dma_start(out=dbg_t, in_=t_full[:]), sOut, 16)

            @block.gpsimd
            def _(g):
                g.wait_ge(sOut.sem, sOut.n)

    return nc


_BUILD_CACHE = {}


def _get_nc():
    if "nc" not in _BUILD_CACHE:
        _BUILD_CACHE["nc"] = build()
    return _BUILD_CACHE["nc"]


def kernel(L0_attn_x, L0_attn_W_fwd, L0_attn_W_out, L0_ffn_sg, L0_ffn_su,
           L0_ffn_sd, L0_ffn_rg, L0_ffn_ru, L0_ffn_rd, L0_ffn_wr, L0_ffn_rb):
    x = np.ascontiguousarray(np.asarray(L0_attn_x, dtype=np.float32))
    Wf = np.asarray(L0_attn_W_fwd, np.float32)
    Wo = np.asarray(L0_attn_W_out, np.float32)
    sg = np.asarray(L0_ffn_sg, np.float32)
    su = np.asarray(L0_ffn_su, np.float32)
    sd = np.asarray(L0_ffn_sd, np.float32)
    rg = np.asarray(L0_ffn_rg, np.float32)
    ru = np.asarray(L0_ffn_ru, np.float32)
    rd = np.asarray(L0_ffn_rd, np.float32)
    wr = np.asarray(L0_ffn_wr, np.float32)
    rb = np.asarray(L0_ffn_rb, np.float32)

    wf_pk = np.ascontiguousarray(
        Wf.reshape(3, 16, 128, 8, 128).transpose(4, 1, 0, 3, 2).reshape(128, -1))
    wo_pk = np.ascontiguousarray(
        Wo.reshape(2, 512, 16, 128).transpose(3, 2, 0, 1).reshape(128, -1))

    def pack_gu(w):
        return w.reshape(NC_I, 128, 8, 128).transpose(3, 0, 2, 1).reshape(128, -1)

    def pack_d(w):
        return w.T.reshape(NC_I, 128, 1024).transpose(1, 0, 2).reshape(128, -1)

    g_pk = np.ascontiguousarray(
        np.stack([pack_gu(rg[e]) for e in range(7)] + [pack_gu(sg)], 0))
    u_pk = np.ascontiguousarray(
        np.stack([pack_gu(ru[e]) for e in range(7)] + [pack_gu(su)], 0))
    d_pk = np.ascontiguousarray(
        np.stack([pack_d(rd[e]) for e in range(7)] + [pack_d(sd)], 0))
    wr_pk = np.ascontiguousarray(
        wr.T.reshape(8, 128, 7).transpose(1, 0, 2).reshape(128, 56))
    rb_pk = np.ascontiguousarray(rb.reshape(1, 7))
    eye_pk = np.eye(128, dtype=np.float32)

    nc = _get_nc()
    shared = {"wf_p": wf_pk, "wo_p": wo_pk, "g_p": g_pk, "u_p": u_pk,
              "d_p": d_pk, "wr_p": wr_pk, "rb_p": rb_pk, "eye_p": eye_pk}
    in_maps = []
    for c in range(8):
        xc = x[2 * c:2 * c + 2].reshape(TOK, D)
        m = dict(shared)
        m["x_tm"] = np.ascontiguousarray(xc)
        m["x_fm"] = np.ascontiguousarray(xc.T)
        in_maps.append(m)

    res = run_bass_kernel_spmd(nc, in_maps, core_ids=list(range(8)))
    _BUILD_CACHE["res"] = res
    y = np.stack([np.asarray(r["y_out"]) for r in res.results], 0)
    return y.reshape(16, 2048, 1024)


# version 28
# speedup vs baseline: 1.1105x; 1.0041x over previous
import contextlib
import numpy as np
import concourse.bass as bass
import concourse.mybir as mybir
from concourse.bass_utils import run_bass_kernel_spmd

F32 = mybir.dt.float32
BF16 = mybir.dt.bfloat16
AF = mybir.ActivationFunctionType
ALU = mybir.AluOpType
AX = mybir.AxisListType

D = 1024
HID = 2048
I_FF = 2816
TOK = 4096
NT_A = 8
TT_A = 512
NT_B = 4
TT_B = 1024
NC_D = 8
NC_H = 16
NC_I = 22
NEG_INF = -1e30
EPS = 1e-5


class Ctr:
    __slots__ = ("sem", "n")

    def __init__(self, sem):
        self.sem = sem
        self.n = 0


def inc(ins, c, k=1):
    ins.then_inc(c.sem, k)
    c.n += k
    return c.n


DEBUG = False


def build():
    nc = bass.Bass("TRN2", target_bir_lowering=False)

    x_tm = nc.dram_tensor("x_tm", [TOK, D], F32, kind="ExternalInput").ap()
    x_fm = nc.dram_tensor("x_fm", [D, TOK], F32, kind="ExternalInput").ap()
    wf_p = nc.dram_tensor("wf_p", [128, 49152], F32, kind="ExternalInput").ap()
    wo_p = nc.dram_tensor("wo_p", [128, 16384], F32, kind="ExternalInput").ap()
    g_p = nc.dram_tensor("g_p", [8, 128, NC_I * 1024], F32, kind="ExternalInput").ap()
    u_p = nc.dram_tensor("u_p", [8, 128, NC_I * 1024], F32, kind="ExternalInput").ap()
    d_p = nc.dram_tensor("d_p", [8, 128, NC_I * 1024], F32, kind="ExternalInput").ap()
    wr_p = nc.dram_tensor("wr_p", [128, 56], F32, kind="ExternalInput").ap()
    rb_p = nc.dram_tensor("rb_p", [1, 7], F32, kind="ExternalInput").ap()
    eye_p = nc.dram_tensor("eye_p", [128, 128], F32, kind="ExternalInput").ap()
    y_out = nc.dram_tensor("y_out", [TOK, D], F32, kind="ExternalOutput").ap()
    wf_st = nc.dram_tensor("wf_st", [128, 49152], BF16, kind="Internal").ap()
    wo_st = nc.dram_tensor("wo_st", [128, 16384], BF16, kind="Internal").ap()
    xn_st = nc.dram_tensor("xn_st", [128, NT_A * 8 * TT_A], BF16, kind="Internal").ap()
    y1_st = nc.dram_tensor("y1_st", [TOK, D], F32, kind="Internal").ap()
    if DEBUG:
        dbg_xn = nc.dram_tensor("dbg_xn", [128, 32768], BF16, kind="ExternalOutput").ap()
        dbg_y1 = nc.dram_tensor("dbg_y1", [TOK, D], F32, kind="ExternalOutput").ap()
        dbg_w = nc.dram_tensor("dbg_w", [128, 224], F32, kind="ExternalOutput").ap()
        dbg_h = nc.dram_tensor("dbg_h", [128, 16 * 513], BF16, kind="ExternalOutput").ap()
        dbg_t = nc.dram_tensor("dbg_t", [128, NC_I * 1024], BF16, kind="ExternalOutput").ap()

    with contextlib.ExitStack() as top:
        TE = top.enter_context
        sPre = Ctr(TE(nc.semaphore("sPre")))
        sP0L = [Ctr(TE(nc.semaphore(f"sP0L{i}"))) for i in range(2)]
        sP0S = [Ctr(TE(nc.semaphore(f"sP0S{i}"))) for i in range(2)]
        sXi2 = [Ctr(TE(nc.semaphore(f"sXi{i}"))) for i in range(2)]
        sXn = Ctr(TE(nc.semaphore("sXn")))
        sY1 = Ctr(TE(nc.semaphore("sY1")))
        sWfi = [Ctr(TE(nc.semaphore(f"sWfi{i}"))) for i in range(2)]
        sWo = Ctr(TE(nc.semaphore("sWo")))
        sGu = [Ctr(TE(nc.semaphore(f"sGu{i}"))) for i in range(2)]
        sDd = [Ctr(TE(nc.semaphore(f"sDd{i}"))) for i in range(2)]
        sBio = Ctr(TE(nc.semaphore("sBio")))
        sOut = Ctr(TE(nc.semaphore("sOut")))
        vT = Ctr(TE(nc.semaphore("vT")))
        vV = Ctr(TE(nc.semaphore("vV")))
        vS = Ctr(TE(nc.semaphore("vS")))
        vP = Ctr(TE(nc.semaphore("vP")))

        pb = [TE(nc.psum_tensor(f"pb{i}", [128, 512], F32)) for i in range(8)]

        eye_s = TE(nc.sbuf_tensor("eye_s", [128, 128], F32))
        ones_r = TE(nc.sbuf_tensor("ones_r", [1, 128], F32))
        ones_c = TE(nc.sbuf_tensor("ones_c", [128, 1], F32))
        wr_s = TE(nc.sbuf_tensor("wr_s", [128, 56], F32))
        rb_r = TE(nc.sbuf_tensor("rb_r", [1, 7], F32))
        s_row = TE(nc.sbuf_tensor("s_row", [1, 7], F32))
        s_bc = TE(nc.sbuf_tensor("s_bc", [128, 7], F32))
        rb_bc = TE(nc.sbuf_tensor("rb_bc", [128, 7], F32))
        w_sb = TE(nc.sbuf_tensor("w_sb", [128, 224], F32))
        zcol = TE(nc.sbuf_tensor("zcol", [128, 1], F32))
        block = TE(nc.Block())

        @block.gpsimd
        def _(g):
            g.memset(ones_r[:], 1.0)
            g.memset(zcol[:], 0.0)
            mm = g.memset(ones_c[:], 1.0)
            inc(mm, vP)
        const_done = vP.n

        @block.sync
        def _(s):
            inc(s.dma_start(out=eye_s[:], in_=eye_p), sPre, 16)
            inc(s.dma_start(out=wr_s[:], in_=wr_p), sPre, 16)
            inc(s.dma_start(out=rb_r[:], in_=rb_p), sPre, 16)
        pre_dma = sPre.n

        with contextlib.ExitStack() as ph0:
            E0 = ph0.enter_context
            p0_stage = E0(nc.sbuf_tensor("p0_stage", [128, 8192], F32))
            p0_bf = E0(nc.sbuf_tensor("p0_bf", [128, 8192], BF16))
            p0_chunks = [(wf_p, wf_st, i * 4096) for i in range(12)] + \
                        [(wo_p, wo_st, i * 4096) for i in range(4)]
            p0_cast = []
            p0_dma = []
            p0_store = []
            for k, (src, dst, off) in enumerate(p0_chunks):
                o = (k % 2) * 4096

                @block.sync
                def _(s, k=k, o=o, src=src, off=off):
                    if k >= 2:
                        s.wait_ge(vV.sem, p0_cast[k - 2])
                    inc(s.dma_start(out=p0_stage[:, o:o + 4096],
                                    in_=src[:, off:off + 4096]), sP0L[k % 2], 16)
                    p0_dma.append(sP0L[k % 2].n)

                @block.vector
                def _(v, k=k, o=o):
                    v.wait_ge(sP0L[k % 2].sem, p0_dma[k])
                    if k >= 2:
                        v.wait_ge(sP0S[k % 2].sem, p0_store[k - 2])
                    mm = v.tensor_copy(out=p0_bf[:, o:o + 4096],
                                       in_=p0_stage[:, o:o + 4096])
                    p0_cast.append(inc(mm, vV))

                @block.gpsimd
                def _(g, k=k, o=o, dst=dst, off=off):
                    g.wait_ge(vV.sem, p0_cast[k])
                    inc(g.dma_start(out=dst[:, off:off + 4096],
                                    in_=p0_bf[:, o:o + 4096]), sP0S[k % 2], 16)
                    p0_store.append(sP0S[k % 2].n)
            p0_store_all = (sP0S[0].n, sP0S[1].n)

            @block.tensor
            def _(t):
                t.wait_ge(sPre.sem, pre_dma)
                t.wait_ge(vP.sem, const_done)
                for c in range(8):
                    t.matmul(out=pb[6][0:1, 0:7], lhsT=ones_c[:],
                             rhs=wr_s[:, c * 7:c * 7 + 7],
                             start=(c == 0), stop=(c == 7))
                mm = t.matmul(out=pb[7][:, 0:7], lhsT=ones_r[:], rhs=rb_r[0:1, 0:7],
                              start=True, stop=True)
                inc(mm, vT)
            pre_t1 = vT.n

            @block.vector
            def _(v):
                v.wait_ge(vT.sem, pre_t1)
                v.tensor_copy(out=s_row[:], in_=pb[6][0:1, 0:7])
                mm = v.tensor_copy(out=rb_bc[:], in_=pb[7][:, 0:7])
                inc(mm, vV)
            pre_v1 = vV.n

            @block.tensor
            def _(t):
                t.wait_ge(vV.sem, pre_v1)
                mm = t.matmul(out=pb[7][:, 0:7], lhsT=ones_r[:], rhs=s_row[0:1, 0:7],
                              start=True, stop=True)
                inc(mm, vT)
            pre_t2 = vT.n

            @block.vector
            def _(v):
                v.wait_ge(vT.sem, pre_t2)
                mm = v.tensor_copy(out=s_bc[:], in_=pb[7][:, 0:7])
                inc(mm, vV)

        with contextlib.ExitStack() as ph_a:
            E = ph_a.enter_context
            xtm = [E(nc.sbuf_tensor(f"xtm{i}", [128, 4096], F32)) for i in range(2)]
            xfm = [E(nc.sbuf_tensor(f"xfm{i}", [128, 4096], F32)) for i in range(2)]
            xnfm = E(nc.sbuf_tensor("xnfm", [128, 4096], BF16))
            h_bf = E(nc.sbuf_tensor("h_bf", [128, NC_H * 513], BF16))
            y_sb = E(nc.sbuf_tensor("y_sb", [128, 4096], F32))
            wfb = [E(nc.sbuf_tensor(f"wfb{i}", [128, 3072], BF16)) for i in range(2)]
            wob16 = E(nc.sbuf_tensor("wob16", [128, 16 * 512], BF16))
            sq_sc = E(nc.sbuf_tensor("sq_sc", [128, 2048], F32))
            negm = E(nc.sbuf_tensor("negm", [128, 4], F32))
            sums = E(nc.sbuf_tensor("sums", [128, 4], F32))
            rstd = E(nc.sbuf_tensor("rstd", [128, 4], F32))
            tc1 = E(nc.sbuf_tensor("tc1", [128, 4], F32))
            nm_row = E(nc.sbuf_tensor("nm_row", [1, 512], F32))
            rs_row = E(nc.sbuf_tensor("rs_row", [1, 512], F32))
            l8 = E(nc.sbuf_tensor("l8", [128, 32], F32))
            m8 = E(nc.sbuf_tensor("m8", [128, 32], F32))
            nm1 = E(nc.sbuf_tensor("nm1", [128, 4], F32))
            dty = E(nc.sbuf_tensor("dty", [128, 4], F32))
            rdE = E(nc.sbuf_tensor("rdE", [128, 4], F32))
            e28 = E(nc.sbuf_tensor("e28", [128, 28], F32))
            mk28 = E(nc.sbuf_tensor("mk28", [128, 28], F32))
            asc = E(nc.sbuf_tensor("asc", [128, 2048], F32))
            easb = [E(nc.sbuf_tensor(f"easb{i}", [128, 512], F32)) for i in range(2)]
            ebsb = [E(nc.sbuf_tensor(f"ebsb{i}", [128, 512], F32)) for i in range(2)]
            rzsb = E(nc.sbuf_tensor("rzsb", [128, 512], F32))
            ezsb = [E(nc.sbuf_tensor(f"ezsb{i}", [128, 512], F32)) for i in range(2)]
            hrsb = [E(nc.sbuf_tensor(f"hrsb{i}", [128, 512], F32)) for i in range(2)]
            densb = E(nc.sbuf_tensor("densb", [128, 512], F32))
            rdenb = E(nc.sbuf_tensor("rdenb", [128, 512], F32))
            isb = [E(nc.sbuf_tensor(f"isb{i}", [128, 512], F32)) for i in range(2)]
            den2b = E(nc.sbuf_tensor("den2b", [128, 512], F32))
            rd2b = [E(nc.sbuf_tensor(f"rd2b{i}", [128, 512], F32)) for i in range(2)]
            szsb = [E(nc.sbuf_tensor(f"szsb{i}", [128, 512], F32)) for i in range(2)]
            gsb = E(nc.sbuf_tensor("gsb", [128, 512], F32))
            f_bu = [E(nc.sbuf_tensor(f"f_bu{i}", [128, 513], F32)) for i in range(2)]
            v_bu = [E(nc.sbuf_tensor(f"v_bu{i}", [128, 513], F32)) for i in range(2)]

            @block.gpsimd
            def _(g):
                mm = g.memset(l8[:], NEG_INF)
                inc(mm, vP)
                g.memset(f_bu[0][:, 0:1], 0.0)
                mm = g.memset(f_bu[1][:, 0:1], 0.0)
                inc(mm, vP)
            amemset_done = vP.n

            x_loaded = [0] * NT_A
            xtm_free = [0, 0]
            xfm_free = [0, 0]
            xn_ready = [0] * NT_A
            xn_stored = [0] * NT_A
            y1_stored = [0] * NT_A
            wf_dma = {}
            pe_iter = {}
            fg_free = {}
            hid_free = {}
            mark = {}
            hin_w = [0] * 16
            var_done_a = [0] * NT_A
            rstd_done_a = [0] * NT_A
            tp_done_a = [0] * NT_A
            bc_done_a = [0] * NT_A
            rt_done_a = [0] * NT_A
            h_all_a = [0] * NT_A
            wo_pe = {}
            ao_evac = {}

            def emit_xload(t):
                b = t % 2

                @block.gpsimd
                def _(g, t=t, b=b):
                    if t < 2:
                        g.wait_ge(sP0S[0].sem, p0_store_all[0])
                        g.wait_ge(sP0S[1].sem, p0_store_all[1])
                        g.wait_ge(vV.sem, p0_cast[-1])
                    w = max(xtm_free[b], xfm_free[b])
                    if w:
                        g.wait_ge(vV.sem, w)
                    inc(g.dma_start(
                        out=xtm[b][:].rearrange("p (j d) -> p j d", j=4),
                        in_=x_tm[t * TT_A:(t + 1) * TT_A, :]
                        .rearrange("(j p) d -> p j d", p=128)), sXi2[b], 16)
                    inc(g.dma_start(
                        out=xfm[b][:].rearrange("p (c n) -> p c n", c=8),
                        in_=x_fm[:, t * TT_A:(t + 1) * TT_A]
                        .rearrange("(c p) n -> p c n", p=128)), sXi2[b], 16)
                    x_loaded[t] = sXi2[b].n

            def emit_ln(t):
                b = t % 2

                @block.scalar
                def _(s, t=t, b=b):
                    s.wait_ge(sXi2[b].sem, x_loaded[t])
                    mm = None
                    for j in range(4):
                        if j == 2:
                            s.wait_ge(vS.sem, inc(mm, vS))
                        mm = s.activation(out=sq_sc[:, (j % 2) * 1024:
                                              (j % 2 + 1) * 1024],
                                          in_=xtm[b][:, j * 1024:(j + 1) * 1024],
                                          func=AF.Square, accum_out=sums[:, j:j + 1])
                    inc(mm, vS)
                sums_done = vS.n

                @block.vector
                def _(v, t=t, b=b):
                    v.wait_ge(sXi2[b].sem, x_loaded[t])
                    if t > 0:
                        v.wait_ge(vT.sem, tp_done_a[t - 1])
                    mm = None
                    for j in range(4):
                        mm = v.reduce_sum(out=negm[:, j:j + 1],
                                          in_=xtm[b][:, j * 1024:(j + 1) * 1024],
                                          axis=AX.X, negate=True)
                    v.wait_ge(vV.sem, inc(mm, vV))
                    mm = v.tensor_scalar(out=negm[:], in0=negm[:], scalar1=1.0 / D,
                                         scalar2=None, op0=ALU.mult)
                    v.wait_ge(vV.sem, inc(mm, vV))
                    mm = v.tensor_tensor(out=tc1[:], in0=negm[:], in1=negm[:],
                                         op=ALU.mult)
                    v.wait_ge(vV.sem, inc(mm, vV))
                    v.wait_ge(vS.sem, sums_done)
                    mm = v.scalar_tensor_tensor(out=tc1[:], in0=sums[:],
                                                scalar=1.0 / D, in1=tc1[:],
                                                op0=ALU.mult, op1=ALU.subtract)
                    v.wait_ge(vV.sem, inc(mm, vV))
                    mm = v.tensor_scalar(out=tc1[:], in0=tc1[:], scalar1=EPS,
                                         scalar2=None, op0=ALU.add)
                    inc(mm, vV)
                var_done_a[t] = vV.n

                @block.scalar
                def _(s, t=t):
                    s.wait_ge(vV.sem, var_done_a[t])
                    mm = s.activation(out=rstd[:], in_=tc1[:], func=AF.Ln)
                    s.wait_ge(vS.sem, inc(mm, vS))
                    mm = s.activation(out=rstd[:], in_=rstd[:], func=AF.Exp,
                                      scale=-0.5)
                    inc(mm, vS)
                rstd_done_a[t] = vS.n

            def emit_tpbc(t):
                @block.tensor
                def _(t_, t=t):
                    t_.wait_ge(vS.sem, rstd_done_a[t])
                    t_.wait_ge(vV.sem, var_done_a[t])
                    for j in range(4):
                        t_.transpose(out=pb[6][0:1, j * 128:(j + 1) * 128],
                                     in_=negm[:, j:j + 1], identity=eye_s[:])
                    for j in range(4):
                        mm = t_.transpose(out=pb[7][0:1, j * 128:(j + 1) * 128],
                                          in_=rstd[:, j:j + 1], identity=eye_s[:])
                    inc(mm, vT)
                tp_done_a[t] = vT.n

                @block.vector
                def _(v, t=t):
                    v.wait_ge(vT.sem, tp_done_a[t])
                    v.tensor_copy(out=nm_row[:], in_=pb[6][0:1, :])
                    mm = v.tensor_copy(out=rs_row[:], in_=pb[7][0:1, :])
                    inc(mm, vV)
                rows_done = vV.n

                @block.tensor
                def _(t_, t=t, rows_done=rows_done):
                    t_.wait_ge(vV.sem, rows_done)
                    t_.matmul(out=pb[6][:], lhsT=ones_r[:], rhs=nm_row[:],
                              start=True, stop=True)
                    mm = t_.matmul(out=pb[7][:], lhsT=ones_r[:], rhs=rs_row[:],
                                   start=True, stop=True)
                    inc(mm, vT)
                bc_done_a[t] = vT.n

            def emit_xn(t):
                b = t % 2

                @block.vector
                def _(v, t=t, b=b):
                    v.wait_ge(vT.sem, bc_done_a[t])
                    if t > 0:
                        v.wait_ge(sXn.sem, xn_stored[t - 1])
                    mm = None
                    for grp in range(2):
                        for c in range(grp * 4, grp * 4 + 4):
                            sl = slice(c * TT_A, (c + 1) * TT_A)
                            a_sl = slice((c % 4) * TT_A, (c % 4 + 1) * TT_A)
                            mm = v.tensor_tensor(out=asc[:, a_sl],
                                                 in0=xfm[b][:, sl], in1=pb[6][:],
                                                 op=ALU.add)
                        v.wait_ge(vV.sem, inc(mm, vV))
                        for c in range(grp * 4, grp * 4 + 4):
                            sl = slice(c * TT_A, (c + 1) * TT_A)
                            a_sl = slice((c % 4) * TT_A, (c % 4 + 1) * TT_A)
                            mm = v.tensor_tensor(out=xnfm[:, sl], in0=asc[:, a_sl],
                                                 in1=pb[7][:], op=ALU.mult)
                        if grp == 0:
                            v.wait_ge(vV.sem, inc(mm, vV))
                    inc(mm, vV)
                xn_ready[t] = vV.n

                @block.gpsimd
                def _(g, t=t):
                    g.wait_ge(vV.sem, xn_ready[t])
                    inc(g.dma_start(out=xn_st[:, t * 4096:(t + 1) * 4096],
                                    in_=xnfm[:]), sXn, 16)
                    xn_stored[t] = sXn.n

            def emit_router(t):
                b = t % 2

                @block.tensor
                def _(t_, t=t, b=b):
                    t_.wait_ge(vV.sem, xn_ready[t])
                    mm = None
                    for g in range(4):
                        for c in range(NC_D):
                            mm = t_.matmul(
                                out=pb[6][:, g * 8:g * 8 + 7],
                                lhsT=xfm[b][:, c * TT_A + g * 128:
                                            c * TT_A + g * 128 + 128],
                                rhs=wr_s[:, c * 7:c * 7 + 7],
                                start=(c == 0), stop=(c == NC_D - 1))
                    inc(mm, vT)
                rt_done_a[t] = vT.n

                @block.vector
                def _(v, t=t):
                    v.wait_ge(vT.sem, rt_done_a[t])
                    v.wait_ge(vP.sem, amemset_done)
                    mm = None
                    for g in range(4):
                        mm = v.scalar_tensor_tensor(out=l8[:, g * 8:g * 8 + 7],
                                                    in0=s_bc[:],
                                                    scalar=negm[:, g:g + 1],
                                                    in1=pb[6][:, g * 8:g * 8 + 7],
                                                    op0=ALU.mult, op1=ALU.add)
                    v.wait_ge(vV.sem, inc(mm, vV))
                    for g in range(4):
                        mm = v.tensor_single_scalar(out=l8[:, g * 8:g * 8 + 7],
                                                    in_=l8[:, g * 8:g * 8 + 7],
                                                    scalar=rstd[:, g:g + 1],
                                                    op=ALU.mult)
                    v.wait_ge(vV.sem, inc(mm, vV))
                    for g in range(4):
                        mm = v.tensor_tensor(out=l8[:, g * 8:g * 8 + 7],
                                             in0=l8[:, g * 8:g * 8 + 7],
                                             in1=rb_bc[:], op=ALU.add)
                    v.wait_ge(vV.sem, inc(mm, vV))
                    for g in range(4):
                        mm = v.max(out=m8[:, g * 8:g * 8 + 8],
                                   in_=l8[:, g * 8:g * 8 + 8])
                    v.wait_ge(vV.sem, inc(mm, vV))
                    for g in range(4):
                        ls = slice(g * 8, g * 8 + 7)
                        v.tensor_scalar(out=nm1[:, g:g + 1],
                                        in0=m8[:, g * 8:g * 8 + 1],
                                        scalar1=-1.0, scalar2=None, op0=ALU.mult)
                        v.tensor_tensor(out=dty[:, g:g + 1],
                                        in0=m8[:, g * 8 + 1:g * 8 + 2],
                                        in1=m8[:, g * 8:g * 8 + 1], op=ALU.subtract)
                        mm = v.tensor_single_scalar(out=mk28[:, g * 7:g * 7 + 7],
                                                    in_=l8[:, ls],
                                                    scalar=m8[:, g * 8 + 1:g * 8 + 2],
                                                    op=ALU.is_ge)
                    inc(mm, vV)
                rt_v = vV.n

                @block.scalar
                def _(s, t=t, rt_v=rt_v):
                    s.wait_ge(vV.sem, rt_v)
                    for g in range(4):
                        s.activation(out=e28[:, g * 7:g * 7 + 7],
                                     in_=l8[:, g * 8:g * 8 + 7], func=AF.Exp,
                                     bias=nm1[:, g:g + 1])
                    mm = s.activation(out=rdE[:], in_=dty[:], func=AF.Exp)
                    inc(mm, vS)
                rt_s = vS.n

                @block.vector
                def _(v, t=t, rt_s=rt_s):
                    v.wait_ge(vS.sem, rt_s)
                    mm = v.tensor_scalar(out=rdE[:], in0=rdE[:], scalar1=1.0,
                                         scalar2=None, op0=ALU.add)
                    v.wait_ge(vV.sem, inc(mm, vV))
                    mm = v.reciprocal(out=rdE[:], in_=rdE[:])
                    v.wait_ge(vV.sem, inc(mm, vV))
                    mm = None
                    for g in range(4):
                        mm = v.tensor_single_scalar(out=e28[:, g * 7:g * 7 + 7],
                                                    in_=e28[:, g * 7:g * 7 + 7],
                                                    scalar=rdE[:, g:g + 1],
                                                    op=ALU.mult)
                    v.wait_ge(vV.sem, inc(mm, vV))
                    mm = None
                    for g in range(4):
                        es = slice(g * 7, g * 7 + 7)
                        wc = (t * 4 + g) * 7
                        mm = v.tensor_tensor(out=w_sb[:, wc:wc + 7], in0=e28[:, es],
                                             in1=mk28[:, es], op=ALU.mult)
                    inc(mm, vV)
                xfm_free[b] = vV.n

            def emit_gates_iter(t, hp):
                idx = t * 16 + hp
                wbi = idx % 2
                trio = 3 * (idx % 2)

                @block.sync
                def _(s, idx=idx, wbi=wbi, hp=hp):
                    if idx == 0:
                        s.wait_ge(sP0S[0].sem, p0_store_all[0])
                        s.wait_ge(sP0S[1].sem, p0_store_all[1])
                    if idx >= 2:
                        s.wait_ge(vT.sem, pe_iter[idx - 2][2])
                    inc(s.dma_start(out=wfb[wbi][:],
                                    in_=wf_st[:, hp * 3072:hp * 3072 + 3072]),
                        sWfi[wbi], 16)
                    wf_dma[idx] = sWfi[wbi].n

                @block.tensor
                def _(t_, t=t, hp=hp, idx=idx, wbi=wbi, trio=trio):
                    t_.wait_ge(sWfi[wbi].sem, wf_dma[idx])
                    if hp == 0:
                        t_.wait_ge(vV.sem, xn_ready[t])
                    if hp <= 1 and t > 0:
                        t_.wait_ge(vV.sem, ao_evac[(t - 1, 1)])
                    if idx >= 2:
                        t_.wait_ge(vS.sem, fg_free[idx - 2])
                        t_.wait_ge(vV.sem, hid_free[idx - 2])
                        t_.wait_ge(vS.sem, mark[(idx - 2, "rd2")])
                    marks = []
                    for q in range(3):
                        mm = None
                        for c in range(NC_D):
                            mm = t_.matmul(
                                out=pb[trio + q][:],
                                lhsT=wfb[wbi][:, (q * 8 + c) * 128:
                                              (q * 8 + c) * 128 + 128],
                                rhs=xnfm[:, c * TT_A:(c + 1) * TT_A],
                                start=(c == 0), stop=(c == NC_D - 1))
                        marks.append(inc(mm, vT))
                    pe_iter[idx] = tuple(marks)

                q0, q1v, q2 = pe_iter[idx]

                @block.scalar
                def _(s, t=t, hp=hp, idx=idx, trio=trio, q0=q0, q1v=q1v, q2=q2):
                    par = idx % 2
                    if idx > 1:
                        s.wait_ge(vV.sem, mark[(idx - 2, "sgz")])
                        s.wait_ge(vP.sem, mark[(idx - 2, "g")])
                    s.wait_ge(vT.sem, q0)
                    s.activation(out=easb[par][:], in_=pb[trio][:], func=AF.Exp,
                                 scale=-1.0)
                    s.wait_ge(vT.sem, q1v)
                    mm = s.activation(out=ebsb[par][:], in_=pb[trio + 1][:],
                                      func=AF.Exp, scale=-1.0)
                    fg_free[idx] = inc(mm, vS)
                    s.wait_ge(vT.sem, q2)
                    mm = s.activation(out=rzsb[:], in_=pb[trio + 2][:], func=AF.Relu,
                                      scale=-1.0)
                    s.wait_ge(vS.sem, inc(mm, vS))
                    s.activation(out=ezsb[par][:], in_=rzsb[:], func=AF.Exp,
                                 scale=-1.0)
                    s.activation(out=den2b[:], in_=ezsb[par][:], func=AF.Ln,
                                 bias=1.0)
                    mm = s.activation(out=rd2b[par][:], in_=den2b[:], func=AF.Exp,
                                      scale=-1.0)
                    mark[(idx, "rd2")] = inc(mm, vS)
                    if idx > 1:
                        s.wait_ge(vV.sem, mark[(idx - 2, "scan")])
                    if t == 0 or t == 4:
                        s.wait_ge(vP.sem, const_done)
                        mm = s.copy(out=v_bu[idx % 2][:, 0:1], in_=zcol[:])
                    else:
                        if hin_w[hp]:
                            s.wait_ge(vV.sem, hin_w[hp])
                        mm = s.copy(out=v_bu[idx % 2][:, 0:1],
                                    in_=h_bf[:, hp * 513 + 512:hp * 513 + 513])
                    mark[(idx, "carry")] = inc(mm, vS)

                @block.vector
                def _(v, idx=idx, trio=trio, q2=q2):
                    par = idx % 2
                    if idx > 1:
                        v.wait_ge(vP.sem, mark[(idx - 2, "v")])
                    v.wait_ge(vS.sem, fg_free[idx])
                    mm = v.scalar_tensor_tensor(out=densb[:], in0=easb[par][:],
                                                scalar=2.0, in1=ebsb[par][:],
                                                op0=ALU.add, op1=ALU.add)
                    v.wait_ge(vV.sem, inc(mm, vV))
                    mm = v.reciprocal(out=rdenb[:], in_=densb[:])
                    v.wait_ge(vV.sem, inc(mm, vV))
                    mm = v.scalar_tensor_tensor(out=f_bu[par][:, 1:513],
                                                in0=ebsb[par][:],
                                                scalar=1.0, in1=rdenb[:],
                                                op0=ALU.add, op1=ALU.mult)
                    mark[(idx, "f")] = inc(mm, vV)
                    mm = v.scalar_tensor_tensor(out=isb[par][:], in0=easb[par][:],
                                                scalar=1.0, in1=rdenb[:],
                                                op0=ALU.add, op1=ALU.mult)
                    mark[(idx, "i")] = inc(mm, vV)
                    v.wait_ge(vT.sem, q2)
                    mm = v.tensor_scalar(out=hrsb[par][:], in0=pb[trio + 2][:],
                                         scalar1=0.0, scalar2=None, op0=ALU.max)
                    hid_free[idx] = inc(mm, vV)
                    mark[(idx, "sgz")] = vV.n

                @block.gpsimd
                def _(g, t=t, hp=hp, idx=idx):
                    par = idx % 2
                    g.wait_ge(vV.sem, hid_free[idx])
                    g.wait_ge(vS.sem, mark[(idx, "rd2")])
                    mm = g.tensor_tensor(out=szsb[par][:], in0=ezsb[par][:],
                                         in1=rd2b[par][:], op=ALU.mult)
                    cnt = inc(mm, vP)
                    g.wait_ge(vP.sem, cnt)
                    mm = g.tensor_tensor(out=gsb[:], in0=szsb[par][:],
                                         in1=hrsb[par][:], op=ALU.add)
                    mark[(idx, "g")] = inc(mm, vP)
                    g.wait_ge(vP.sem, mark[(idx, "g")])
                    if idx > 1:
                        g.wait_ge(vV.sem, mark[(idx - 2, "scan")])
                    mm = g.tensor_tensor(out=v_bu[par][:, 1:513], in0=isb[par][:],
                                         in1=gsb[:], op=ALU.mult)
                    mark[(idx, "v")] = inc(mm, vP)

                def emit_scan(j, t=t):
                    jp = j % 16

                    @block.vector
                    def _(v, j=j, jp=jp):
                        v.wait_ge(vP.sem, mark[(j, "v")])
                        v.wait_ge(vS.sem, mark[(j, "carry")])
                        mm = v.tensor_tensor_scan(out=h_bf[:, jp * 513:
                                                         (jp + 1) * 513],
                                                  data0=f_bu[j % 2][:],
                                                  data1=v_bu[j % 2][:], initial=0.0,
                                                  op0=ALU.mult, op1=ALU.add)
                        mark[(j, "scan")] = inc(mm, vV)
                        hin_w[jp] = mark[(j, "scan")]

                if hp > 0:
                    emit_scan(idx - 1)
                if hp == 15:
                    emit_scan(idx)
                    h_all_a[t] = vV.n

            def emit_attn_out(t, half):
                b = t % 2

                @block.sync
                def _(s, t=t, half=half):
                    if half == 0:
                        if t > 0:
                            s.wait_ge(vT.sem, wo_pe[(t - 1, 1)])
                        else:
                            s.wait_ge(sP0S[0].sem, p0_store_all[0])
                            s.wait_ge(sP0S[1].sem, p0_store_all[1])
                    else:
                        s.wait_ge(vT.sem, wo_pe[(t, 0)])
                    for hc in range(NC_H):
                        inc(s.dma_start(
                            out=wob16[:, hc * 512:(hc + 1) * 512],
                            in_=wo_st[:, (hc * 2 + half) * 512:
                                      (hc * 2 + half) * 512 + 512]), sWo, 16)
                    wf_dma[(t, half, "wo")] = sWo.n

                @block.tensor
                def _(t_, t=t, half=half):
                    t_.wait_ge(sWo.sem, wf_dma[(t, half, "wo")])
                    if half == 0:
                        t_.wait_ge(vS.sem, mark[(t * 16 + 15, "rd2")])
                        t_.wait_ge(vV.sem, hid_free[t * 16 + 15])
                        if t > 0:
                            t_.wait_ge(vV.sem, ao_evac[(t - 1, 1)])
                    else:
                        t_.wait_ge(vV.sem, ao_evac[(t, 0)])
                    mm = None
                    for hc in range(NC_H):
                        if half == 0:
                            t_.wait_ge(vV.sem, mark[(t * 16 + hc, "scan")])
                        for g in range(4):
                            mm = t_.matmul(
                                out=pb[g][:],
                                lhsT=h_bf[:, hc * 513 + 1 + g * 128:
                                          hc * 513 + 1 + g * 128 + 128],
                                rhs=wob16[:, hc * 512:(hc + 1) * 512],
                                start=(hc == 0), stop=(hc == NC_H - 1))
                    inc(mm, vT)
                    wo_pe[(t, half)] = vT.n
                ao_pe = vT.n

                @block.vector
                def _(v, t=t, b=b, half=half, ao_pe=ao_pe):
                    v.wait_ge(vT.sem, ao_pe)
                    if t > 0 and half == 0:
                        v.wait_ge(sY1.sem, y1_stored[t - 1])
                    mm = None
                    for g in range(4):
                        sl = slice(g * 1024 + half * 512,
                                   g * 1024 + half * 512 + 512)
                        mm = v.tensor_tensor(out=y_sb[:, sl], in0=xtm[b][:, sl],
                                             in1=pb[g][:], op=ALU.add)
                    ao_evac[(t, half)] = inc(mm, vV)
                if half == 1:
                    xtm_free[b] = vV.n

            def emit_ystore(t):
                @block.gpsimd
                def _(g, t=t):
                    g.wait_ge(vV.sem, ao_evac[(t, 0)])
                    g.wait_ge(vV.sem, ao_evac[(t, 1)])
                    inc(g.dma_start(out=y1_st[t * TT_A:(t + 1) * TT_A, :]
                                    .rearrange("(j p) d -> p j d", p=128),
                                    in_=y_sb[:].rearrange("p (j d) -> p j d", j=4)),
                        sY1, 16)
                    y1_stored[t] = sY1.n

            emit_xload(0)
            emit_xload(1)
            emit_ln(0)
            emit_tpbc(0)
            emit_xn(0)
            emit_router(0)

            for t in range(NT_A):
                for hp in range(16):
                    emit_gates_iter(t, hp)
                if t + 1 < NT_A:
                    emit_ln(t + 1)
                emit_attn_out(t, 0)
                if t + 1 < NT_A:
                    emit_tpbc(t + 1)
                    emit_xn(t + 1)
                emit_attn_out(t, 1)
                if t + 2 < NT_A:
                    emit_xload(t + 2)
                if t + 1 < NT_A:
                    emit_router(t + 1)
                emit_ystore(t)

            if DEBUG:
                @block.sync
                def _(s):
                    s.wait_ge(vS.sem, vS.n)
                    s.wait_ge(vV.sem, vV.n)
                    s.wait_ge(vP.sem, vP.n)
                    s.wait_ge(sXn.sem, sXn.n)
                    s.wait_ge(sY1.sem, sY1.n)
                    inc(s.dma_start(out=dbg_h, in_=h_bf[:]), sOut, 16)
                    inc(s.dma_start(out=dbg_w, in_=w_sb[:]), sOut, 16)
                    inc(s.dma_start(out=dbg_xn, in_=xn_st), sOut, 16)
                    inc(s.dma_start(out=dbg_y1, in_=y1_st), sOut, 16)

            a_V, a_S, a_P, a_T = vV.n, vS.n, vP.n, vT.n
            a_dma = [(sXn, sXn.n), (sY1, sY1.n), (sWo, sWo.n),
                     (sWfi[0], sWfi[0].n), (sWfi[1], sWfi[1].n),
                     (sXi2[0], sXi2[0].n), (sXi2[1], sXi2[1].n),
                     (sP0S[0], sP0S[0].n), (sP0S[1], sP0S[1].n)]

        for eng in ("vector", "scalar", "gpsimd", "tensor", "sync"):
            @getattr(block, eng)
            def _(e):
                e.wait_ge(vV.sem, a_V)
                e.wait_ge(vS.sem, a_S)
                e.wait_ge(vP.sem, a_P)
                e.wait_ge(vT.sem, a_T)
                for c_, n_ in a_dma:
                    e.wait_ge(c_.sem, n_)

        with contextlib.ExitStack() as ph_b:
            E = ph_b.enter_context
            xn_b = E(nc.sbuf_tensor("xn_b", [128, 8 * TT_B], BF16))
            y_b = E(nc.sbuf_tensor("y_b", [128, 8 * 1024], F32))
            t_full = E(nc.sbuf_tensor("t_full", [128, NC_I * 1024], BF16))
            d_all = E(nc.sbuf_tensor("d_all", [128, NC_I * 1024], BF16))
            g_st = [E(nc.sbuf_tensor(f"g_st{i}", [128, 1024], F32)) for i in range(4)]
            u_st = [E(nc.sbuf_tensor(f"u_st{i}", [128, 1024], F32)) for i in range(4)]
            g_bf = [E(nc.sbuf_tensor(f"g_bf{i}", [128, 1024], BF16)) for i in range(4)]
            u_bf = [E(nc.sbuf_tensor(f"u_bf{i}", [128, 1024], BF16)) for i in range(4)]
            d_st = [E(nc.sbuf_tensor(f"d_st{i}", [128, 2048], F32)) for i in range(2)]
            s_sc = [E(nc.sbuf_tensor(f"s_sc{i}", [128, 512], F32)) for i in range(2)]
            sGu4 = [Ctr(TE(nc.semaphore(f"sGu4_{i}"))) for i in range(4)]

            sout_base = sOut.n
            io_done = [0] * NT_B
            gu_dma = {}
            gu_cast = {}
            gu_pe_last = {}
            unit_pe = {}
            unit_silu = {}
            unit_t = {}
            d_dma = {}
            d_cast = {}
            d_marks = []
            ddk = [0]
            t_war = 0
            d_war = 0
            prev_evac3 = 0
            xn_pe_last = [0] * NT_B
            uk = 0
            ick = 0
            dck = 0
            ic_total = 0

            def emit_gu_fetch(ick2, ic2, gsrc, usrc):
                sbi = ick2 % 4

                @block.sync
                def _(s):
                    inc(s.dma_start(out=g_st[sbi][:],
                                    in_=gsrc[:, ic2 * 1024:(ic2 + 1) * 1024]),
                        sGu4[sbi], 16)
                    inc(s.dma_start(out=u_st[sbi][:],
                                    in_=usrc[:, ic2 * 1024:(ic2 + 1) * 1024]),
                        sGu4[sbi], 16)
                    gu_dma[ick2] = sGu4[sbi].n

                @block.scalar
                def _(s):
                    s.wait_ge(sGu4[sbi].sem, gu_dma[ick2])
                    s.copy(out=g_bf[sbi][:], in_=g_st[sbi][:])
                    mm = s.copy(out=u_bf[sbi][:], in_=u_st[sbi][:])
                    gu_cast[ick2] = inc(mm, vS)

            for bt in range(NT_B):
                @block.gpsimd
                def _(g, bt=bt):
                    if bt > 0:
                        g.wait_ge(vT.sem, xn_pe_last[bt - 1])
                        g.wait_ge(sOut.sem, sout_base + 16 * bt)
                        g.wait_ge(vV.sem, y_evac_last)
                    for u in range(2):
                        inc(g.dma_start(
                            out=bass.AP(xn_b, u * 512,
                                        [[8192, 128], [1024, 8], [1, 512]]),
                            in_=xn_st[:, (2 * bt + u) * 4096:(2 * bt + u + 1) * 4096]
                            .rearrange("p (c s) -> p c s", c=8)), sBio, 16)
                    inc(g.dma_start(
                        out=y_b[:].rearrange("p (j d) -> p j d", j=8),
                        in_=y1_st[bt * TT_B:(bt + 1) * TT_B, :]
                        .rearrange("(j p) d -> p j d", p=128)), sBio, 16)
                    io_done[bt] = sBio.n

                srcs = [(g_p[e], u_p[e], d_p[e]) for e in range(8)]

                def glb(e_, ic_):
                    return bt * 8 * NC_I + e_ * NC_I + ic_

                def src_of(k):
                    e_, ic_ = divmod(k % (8 * NC_I), NC_I)
                    return srcs[e_][0], srcs[e_][1], ic_

                if bt == 0:
                    for k in range(2):
                        gs_, us_, ic_ = src_of(k)
                        emit_gu_fetch(k, ic_, gs_, us_)

                for e in range(8):
                    gsrc, usrc, dsrc = srcs[e]

                    def emit_d_dma(dsrc=dsrc, e=e):
                        gdk = ddk[0]
                        dk = gdk % 11
                        dbi = gdk % 2

                        @block.gpsimd
                        def _(g, gdk=gdk, dk=dk, dbi=dbi, dsrc=dsrc):
                            w = d_cast.get(gdk - 2, 0)
                            if w:
                                g.wait_ge(vS.sem, w)
                            inc(g.dma_start(out=d_st[dbi][:],
                                            in_=dsrc[:, dk * 2048:(dk + 1) * 2048]),
                                sDd[dbi], 16)
                            d_dma[gdk] = sDd[dbi].n
                        ddk[0] += 1

                    emit_d_dma()
                    emit_d_dma()

                    for ic in range(NC_I):
                        k2 = glb(e, ic) + 2
                        has_fetch = k2 < 4 * 8 * NC_I
                        if has_fetch:
                            gs_, us_, ic2 = src_of(k2)
                            sb2 = k2 % 4

                            @block.sync
                            def _(s, k2=k2, sb2=sb2, ic2=ic2, gs_=gs_, us_=us_):
                                w = gu_cast.get(k2 - 4, 0)
                                if w:
                                    s.wait_ge(vS.sem, w)
                                inc(s.dma_start(out=g_st[sb2][:],
                                                in_=gs_[:, ic2 * 1024:(ic2 + 1) * 1024]),
                                    sGu4[sb2], 16)
                                inc(s.dma_start(out=u_st[sb2][:],
                                                in_=us_[:, ic2 * 1024:(ic2 + 1) * 1024]),
                                    sGu4[sb2], 16)
                                gu_dma[k2] = sGu4[sb2].n

                        ickg = glb(e, ic)
                        sbi = ickg % 4
                        uk0, uk1 = uk, uk + 1

                        for half in range(2):
                            bx = 2 * ((uk0 + half) % 2)
                            tok0 = half * 512

                            @block.tensor
                            def _(t_, bt=bt, e=e, ic=ic, ickg=ickg, uk=uk0 + half,
                                  sbi=sbi, half=half, bx=bx, tok0=tok0,
                                  pe3=prev_evac3):
                                if half == 0:
                                    t_.wait_ge(vS.sem, gu_cast[ickg])
                                if ic == 0 and half == 0:
                                    t_.wait_ge(sBio.sem, io_done[bt])
                                if ic == 0 and pe3:
                                    t_.wait_ge(vV.sem, pe3)
                                w_s = unit_silu.get(uk - 2, 0)
                                w_t = unit_t.get(uk - 2, 0)
                                if w_s:
                                    t_.wait_ge(vS.sem, w_s)
                                if w_t:
                                    t_.wait_ge(vV.sem, w_t)
                                for c in range(NC_D):
                                    t_.matmul(out=pb[bx][:],
                                              lhsT=g_bf[sbi][:, c * 128:(c + 1) * 128],
                                              rhs=xn_b[:, c * TT_B + tok0:
                                                       c * TT_B + tok0 + 512],
                                              start=(c == 0), stop=(c == NC_D - 1))
                                mm = None
                                for c in range(NC_D):
                                    mm = t_.matmul(out=pb[bx + 1][:],
                                                   lhsT=u_bf[sbi][:,
                                                              c * 128:(c + 1) * 128],
                                                   rhs=xn_b[:, c * TT_B + tok0:
                                                            c * TT_B + tok0 + 512],
                                                   start=(c == 0),
                                                   stop=(c == NC_D - 1))
                                unit_pe[uk] = inc(mm, vT)
                            gu_pe_last[ickg] = vT.n

                        bx0 = 2 * (uk0 % 2)
                        bx1 = 2 * (uk1 % 2)

                        @block.scalar
                        def _(s, bt=bt, e=e, ic=ic, uk0=uk0, uk1=uk1, bx0=bx0,
                              bx1=bx1, has_fetch=has_fetch,
                              k2=k2, dsrc=dsrc, dck=dck):
                            if has_fetch:
                                sb2 = k2 % 4
                                w = gu_pe_last.get(k2 - 4, 0)
                                if w:
                                    s.wait_ge(vT.sem, w)
                                s.wait_ge(sGu4[sb2].sem, gu_dma[k2])
                                s.copy(out=g_bf[sb2][:], in_=g_st[sb2][:])
                                mm = s.copy(out=u_bf[sb2][:], in_=u_st[sb2][:])
                                gu_cast[k2] = inc(mm, vS)
                            if ic % 2 == 0 and ic // 2 < 11:
                                dk = ic // 2
                                gdk = dck
                                dbi = gdk % 2
                                s.wait_ge(sDd[dbi].sem, d_dma[gdk])
                                if dk == 0 and d_war:
                                    s.wait_ge(vT.sem, d_war)
                                mm = s.copy(out=d_all[:, dk * 2048:(dk + 1) * 2048],
                                            in_=d_st[dbi][:])
                                d_cast[gdk] = inc(mm, vS)
                                d_marks.append(gdk)
                            w = unit_t.get(uk0 - 2, 0)
                            if w:
                                s.wait_ge(vV.sem, w)
                            s.wait_ge(vT.sem, unit_pe[uk0])
                            mm = s.activation(out=s_sc[0][:], in_=pb[bx0][:],
                                              func=AF.Silu)
                            unit_silu[uk0] = inc(mm, vS)
                            w = unit_t.get(uk1 - 2, 0)
                            if w:
                                s.wait_ge(vV.sem, w)
                            s.wait_ge(vT.sem, unit_pe[uk1])
                            mm = s.activation(out=s_sc[1][:], in_=pb[bx1][:],
                                              func=AF.Silu)
                            unit_silu[uk1] = inc(mm, vS)
                        if ic % 2 == 0 and ic // 2 < 11:
                            dck += 1
                            if ic // 2 < 9:
                                emit_d_dma()

                        @block.vector
                        def _(v, ic=ic, uk0=uk0, uk1=uk1, bx0=bx0, bx1=bx1,
                              t_war=t_war):
                            v.wait_ge(vS.sem, unit_silu[uk0])
                            if ic == 0 and t_war:
                                v.wait_ge(vT.sem, t_war)
                            mm = v.tensor_tensor(
                                out=t_full[:, ic * 1024:ic * 1024 + 512],
                                in0=s_sc[0][:], in1=pb[bx0 + 1][:], op=ALU.mult)
                            unit_t[uk0] = inc(mm, vV)
                            v.wait_ge(vS.sem, unit_silu[uk1])
                            mm = v.tensor_tensor(
                                out=t_full[:, ic * 1024 + 512:ic * 1024 + 1024],
                                in0=s_sc[1][:], in1=pb[bx1 + 1][:], op=ALU.mult)
                            unit_t[uk1] = inc(mm, vV)
                        uk += 2

                    d_ready = max(d_cast.get(k, 0) for k in d_marks[-11:]) \
                        if d_marks else 0
                    t_last = unit_t[uk - 1]

                    sub2_evac = {}
                    for gg in range(4):
                        quad = 4 if gg % 2 == 0 else 0

                        @block.tensor
                        def _(t_, bt=bt, e=e, gg=gg, quad=quad, d_ready=d_ready,
                              t_last=t_last):
                            if gg == 0:
                                if d_ready:
                                    t_.wait_ge(vS.sem, d_ready)
                            elif gg == 1:
                                t_.wait_ge(vV.sem, t_last)
                            else:
                                t_.wait_ge(vV.sem, sub2_evac[gg - 2])
                            base_uk = (bt * 8 + e) * 2 * NC_I
                            mm = None
                            for ic in range(NC_I):
                                if gg == 0:
                                    t_.wait_ge(vV.sem,
                                               unit_t[base_uk + 2 * ic + 1])
                                for g2 in range(2):
                                    for dh in range(2):
                                        mm = t_.matmul(
                                            out=pb[quad + 2 * g2 + dh][:],
                                            lhsT=t_full[:, ic * 1024 + gg * 256 +
                                                        g2 * 128:
                                                        ic * 1024 + gg * 256 +
                                                        g2 * 128 + 128],
                                            rhs=d_all[:, ic * 1024 + dh * 512:
                                                      ic * 1024 + dh * 512 + 512],
                                            start=(ic == 0), stop=(ic == NC_I - 1))
                            inc(mm, vT)
                        dpe = vT.n

                        @block.vector
                        def _(v, bt=bt, e=e, gg=gg, quad=quad, dpe=dpe,
                              vflush=vV.n):
                            v.wait_ge(vT.sem, dpe)
                            v.wait_ge(vV.sem, vflush)
                            if e == 0 and gg == 0:
                                v.wait_ge(sBio.sem, io_done[bt])
                            mm = None
                            for g2 in range(2):
                                for dh in range(2):
                                    gi = gg * 2 + g2
                                    sl = slice(gi * 1024 + dh * 512,
                                               gi * 1024 + dh * 512 + 512)
                                    if e == 7:
                                        mm = v.tensor_tensor(
                                            out=y_b[:, sl], in0=y_b[:, sl],
                                            in1=pb[quad + 2 * g2 + dh][:],
                                            op=ALU.add)
                                    else:
                                        t_a = 2 * bt + gi // 4
                                        g_a = gi % 4
                                        wcol = (t_a * 4 + g_a) * 7 + e
                                        mm = v.scalar_tensor_tensor(
                                            out=y_b[:, sl],
                                            in0=pb[quad + 2 * g2 + dh][:],
                                            scalar=w_sb[:, wcol:wcol + 1],
                                            in1=y_b[:, sl],
                                            op0=ALU.mult, op1=ALU.add)
                            sub2_evac[gg] = inc(mm, vV)
                    t_war = vT.n
                    d_war = vT.n
                    prev_evac3 = max(sub2_evac[1], sub2_evac[3])
                    y_evac_last = vV.n
                xn_pe_last[bt] = vT.n

                @block.gpsimd
                def _(g, bt=bt, yy=None):
                    g.wait_ge(vV.sem, y_evac_last)
                    inc(g.dma_start(out=y_out[bt * TT_B:(bt + 1) * TT_B, :]
                                    .rearrange("(j p) d -> p j d", p=128),
                                    in_=y_b[:].rearrange("p (j d) -> p j d", j=8)),
                        sOut, 16)

            if DEBUG:
                @block.sync
                def _(s):
                    s.wait_ge(vV.sem, vV.n)
                    inc(s.dma_start(out=dbg_t, in_=t_full[:]), sOut, 16)

            @block.gpsimd
            def _(g):
                g.wait_ge(sOut.sem, sOut.n)

    return nc


_BUILD_CACHE = {}


def _get_nc():
    if "nc" not in _BUILD_CACHE:
        _BUILD_CACHE["nc"] = build()
    return _BUILD_CACHE["nc"]


def kernel(L0_attn_x, L0_attn_W_fwd, L0_attn_W_out, L0_ffn_sg, L0_ffn_su,
           L0_ffn_sd, L0_ffn_rg, L0_ffn_ru, L0_ffn_rd, L0_ffn_wr, L0_ffn_rb):
    x = np.ascontiguousarray(np.asarray(L0_attn_x, dtype=np.float32))
    Wf = np.asarray(L0_attn_W_fwd, np.float32)
    Wo = np.asarray(L0_attn_W_out, np.float32)
    sg = np.asarray(L0_ffn_sg, np.float32)
    su = np.asarray(L0_ffn_su, np.float32)
    sd = np.asarray(L0_ffn_sd, np.float32)
    rg = np.asarray(L0_ffn_rg, np.float32)
    ru = np.asarray(L0_ffn_ru, np.float32)
    rd = np.asarray(L0_ffn_rd, np.float32)
    wr = np.asarray(L0_ffn_wr, np.float32)
    rb = np.asarray(L0_ffn_rb, np.float32)

    wf_pk = np.ascontiguousarray(
        Wf.reshape(3, 16, 128, 8, 128).transpose(4, 1, 0, 3, 2).reshape(128, -1))
    wo_pk = np.ascontiguousarray(
        Wo.reshape(2, 512, 16, 128).transpose(3, 2, 0, 1).reshape(128, -1))

    def pack_gu(w):
        return w.reshape(NC_I, 128, 8, 128).transpose(3, 0, 2, 1).reshape(128, -1)

    def pack_d(w):
        return w.T.reshape(NC_I, 128, 1024).transpose(1, 0, 2).reshape(128, -1)

    g_pk = np.ascontiguousarray(
        np.stack([pack_gu(rg[e]) for e in range(7)] + [pack_gu(sg)], 0))
    u_pk = np.ascontiguousarray(
        np.stack([pack_gu(ru[e]) for e in range(7)] + [pack_gu(su)], 0))
    d_pk = np.ascontiguousarray(
        np.stack([pack_d(rd[e]) for e in range(7)] + [pack_d(sd)], 0))
    wr_pk = np.ascontiguousarray(
        wr.T.reshape(8, 128, 7).transpose(1, 0, 2).reshape(128, 56))
    rb_pk = np.ascontiguousarray(rb.reshape(1, 7))
    eye_pk = np.eye(128, dtype=np.float32)

    nc = _get_nc()
    shared = {"wf_p": wf_pk, "wo_p": wo_pk, "g_p": g_pk, "u_p": u_pk,
              "d_p": d_pk, "wr_p": wr_pk, "rb_p": rb_pk, "eye_p": eye_pk}
    in_maps = []
    for c in range(8):
        xc = x[2 * c:2 * c + 2].reshape(TOK, D)
        m = dict(shared)
        m["x_tm"] = np.ascontiguousarray(xc)
        m["x_fm"] = np.ascontiguousarray(xc.T)
        in_maps.append(m)

    res = run_bass_kernel_spmd(nc, in_maps, core_ids=list(range(8)))
    _BUILD_CACHE["res"] = res
    y = np.stack([np.asarray(r["y_out"]) for r in res.results], 0)
    return y.reshape(16, 2048, 1024)
